# version 22
# baseline (speedup 1.0000x reference)
import sys
if "/opt/trn_rl_repo" not in sys.path:
    sys.path.insert(0, "/opt/trn_rl_repo")
import numpy as np
import concourse.bass as bass
import concourse.mybir as mybir
from concourse.bass_utils import run_bass_kernel_spmd
from concourse.alu_op_type import AluOpType as OP

F32 = mybir.dt.float32
BF16 = mybir.dt.bfloat16
AF = mybir.ActivationFunctionType
AX = mybir.AxisListType.X

MAGIC = 1.5 * 2.0 ** 23
EPS = 1e-5
NT = 32
D = 1024
SK = 4096
H = 4096
NEG = -3.0e38


def build_program(stage=4):
    nc = bass.Bass("TRN2", target_bir_lowering=False, detect_race_conditions=False)

    Xd = nc.dram_tensor("x", [SK, D], F32, kind="ExternalInput").ap()
    KTd = nc.dram_tensor("kT", [D, SK], F32, kind="ExternalInput").ap()
    Vd = nc.dram_tensor("v", [SK, D], F32, kind="ExternalInput").ap()
    W1Td = nc.dram_tensor("w1T", [D, H], F32, kind="ExternalInput").ap()
    W2Td = nc.dram_tensor("w2T", [H, D], F32, kind="ExternalInput").ap()
    B1d = nc.dram_tensor("b1bc", [128, H], F32, kind="ExternalInput").ap()
    B2d = nc.dram_tensor("b2bc", [128, D], F32, kind="ExternalInput").ap()
    Gd = nc.dram_tensor("gbc", [128, D], F32, kind="ExternalInput").ap()
    Bd = nc.dram_tensor("bbc", [128, D], F32, kind="ExternalInput").ap()
    OUTd = nc.dram_tensor("out", [SK, D], F32, kind="ExternalOutput").ap()

    cur = [(nc.sbuf_base + 63) // 64 * 64]
    offs = {}

    def sb(name, shape, dt, alias_at=None):
        nbytes = int(np.prod(shape[1:])) * (2 if dt == BF16 else 4)
        off = cur[0] if alias_at is None else alias_at
        t = nc.alloc_sbuf_tensor_at(name, shape, dt, offset=off).ap()
        if alias_at is None:
            cur[0] += (nbytes + 31) // 32 * 32
        offs[name] = off
        return t

    KW1 = sb("KW1", [128, 8, SK], BF16)
    VW2 = sb("VW2", [128, 32, D], BF16)
    big0 = sb("big0", [128, SK], F32)
    bc16 = sb("bc16", [128, SK], F32)
    P8 = sb("P8", [128, SK], BF16)
    gbc = sb("gbc_s", [128, D], F32, alias_at=offs["P8"])
    bbc = sb("bbc_s", [128, D], F32, alias_at=offs["P8"] + 4096)
    stg45 = sb("stg45", [128, 2, 1024], F32, alias_at=offs["P8"])

    def qslot(c):
        s = c % 6
        return big0[:, s * 1024:(s + 1) * 1024] if s < 4 else stg45[:, s - 4, :]

    QPAT = ("d_pb", "d_pb3", "d_pb", "d_pb3", "d_pb2")

    def qname(c):
        return QPAT[c % 5]

    def qpos(c):
        return sum(1 for j in range(c + 1) if QPAT[j % 5] == QPAT[c % 5])

    QSEMS = {"d_pb": ("d_pb", "d_pba", "d_pbb"),
             "d_pb3": ("d_pb3", "d_p3a", "d_p3b"),
             "d_pb2": ("d_pb2",)}

    def qsem(c):
        q = qname(c)
        p = qpos(c) - 1
        sems = QSEMS[q]
        return sems[p % len(sems)], 16 * (p // len(sems) + 1)
    PT8 = sb("PT8", [128, SK], BF16)
    x2q = sb("x2q", [128, 8, 512], BF16, alias_at=offs["PT8"])
    xbuf = sb("xbuf", [128, 3, D], F32)
    sgbuf = sb("sgbuf", [128, 2, D], F32)
    sgbf = sb("sgbf", [128, 2, D], BF16, alias_at=offs["sgbuf"])
    x2T = sb("x2T", [128, 2, 512], BF16, alias_at=offs["sgbuf"] + 4096)
    xng = sb("xng", [128, D], F32)
    gtb = sb("gtb", [128, D], BF16, alias_at=offs["sgbuf"] + 6144)
    absg = sb("absg", [128, 2, D], BF16, alias_at=offs["xng"])
    xnb = sb("xnb", [128, D], BF16)
    xnT = sb("xnT", [128, D], BF16)
    scrb = sb("scrb", [128, D], BF16)
    b2bc = sb("b2bc_s", [128, D], BF16, alias_at=offs["scrb"])
    stP = sb("stP", [128, NT, 4], F32)
    stA = sb("stA", [128, 2, 9], F32)
    stB = sb("stB", [128, 2, 9], F32)
    mpt = sb("mpt", [128, 8], F32)
    accw = sb("accw", [128, 2], F32)
    wpart = sb("wpart", [128, 64], F32)
    wtmp = sb("wtmp", [128, 4], F32)
    msw = sb("msw", [128, 4], F32)
    mwbc = msw[:, 0:2]
    swbc = msw[:, 2:4]
    onesm = sb("onesm", [128, 2], F32)
    ones = onesm[:, 0:1]
    magicc = onesm[:, 1:2]
    onesr = sb("onesr", [1, 128], F32)
    ident = sb("ident", [128, 128], BF16)
    assert cur[0] <= 229376, f"SBUF over budget: {cur[0]}"

    xn_ = xng[:, 0:D]
    gt_ = gtb[:, 0:D]
    SS = [big0, bc16]

    psA = nc.alloc_psum_tensor("psA", [128, 2, 512], F32).ap()
    psV = nc.alloc_psum_tensor("psV", [128, 1024], F32).ap()
    psT = nc.alloc_psum_tensor("psT", [128, 2, 1024], BF16).ap()
    psX = nc.alloc_psum_tensor("psX", [128, 1024], BF16).ap()
    psR = nc.alloc_psum_tensor("psR", [128, 16], F32).ap()

    semnames = ("d_x d_pa d_v d_wa d_s d_pb d_bc d_sr d_o d_cat "
                "a_ln a_xa a_xt a_xe a_sd a_se a_me a_pe a_rl a_pt a_pv a_vd a_ve a_wr "
                "g_init v_init p_ln4 p_x4 p_bc p_bce p_kc "
                "t_pr t_vw t_pb t_sw b_q1 b_q2 b_qv "
                "b_l1a b_l2 b_h2 b_q1a b_xq b_xt b_xe b_fd b_fe b_ge b_q2s "
                "b_q2c b_pt b_pv b_vd b_y2 b_sl b_fin a_sq b_sq p_sq4 p_s4a d_pb2 "
                "d_pb3 d_pba d_pbb d_p3a d_p3b d_ka d_kb d_kc d_va d_vb").split()
    S = {n: nc.alloc_semaphore(n) for n in semnames}

    def w(eng, name, v):
        if v > 0:
            eng.wait_ge(S[name], v)

    A_SUM, A_SSQ, A_MU, A_W, A_Y, A_T, A_NM, A_L, A_RL = 0, 1, 2, 3, 4, 5, 6, 7, 8
    B_DS2, B_T, B_AM1, B_M1, B_SX1, B_DS1, B_AM2, B_M2, B_SX2 = range(9)

    def newton_rsqrt(vector, st, par, out_rs, out_negmurs, npart=128, iters=3,
                 out_rs12=None, out_nmr12=None):
        sl = lambda k: st[0:npart, par, k:k + 1]
        dr = vector.drain
        dr()
        vector.tensor_scalar(sl(A_MU), sl(A_SUM), 1.0 / D, None, OP.mult)
        vector.tensor_scalar(sl(A_W), sl(A_SSQ), 1.0 / D, None, OP.mult)
        dr()
        vector.tensor_tensor(sl(A_T), sl(A_MU), sl(A_MU), OP.mult)
        dr()
        vector.tensor_tensor(sl(A_W), sl(A_W), sl(A_T), OP.subtract)
        dr()
        vector.tensor_scalar(sl(A_W), sl(A_W), EPS, None, OP.add)
        dr()
        vector.tensor_scalar(sl(A_Y), sl(A_W), -0.5, 1.5, OP.mult, OP.add)
        dr()
        for _ in range(iters - 1):
            vector.tensor_tensor(sl(A_T), sl(A_Y), sl(A_Y), OP.mult)
            dr()
            vector.tensor_tensor(sl(A_T), sl(A_T), sl(A_W), OP.mult)
            dr()
            vector.tensor_scalar(sl(A_T), sl(A_T), -0.5, 1.5, OP.mult, OP.add)
            dr()
            vector.tensor_tensor(sl(A_Y), sl(A_Y), sl(A_T), OP.mult)
            dr()
        vector.tensor_copy(out_rs, sl(A_Y))
        vector.tensor_tensor(sl(A_T), sl(A_MU), sl(A_Y), OP.mult)
        dr()
        last = vector.tensor_scalar(out_negmurs, sl(A_T), -1.0, None, OP.mult)
        if out_rs12 is None:
            return last
        vector.reciprocal(sl(A_MU), sl(A_W))
        dr()
        vector.tensor_scalar(sl(A_MU), sl(A_MU), -EPS, 1.0, OP.mult, OP.add)
        dr()
        vector.tensor_scalar(sl(A_MU), sl(A_MU), -0.5, 1.5 - 0.5 * EPS, OP.mult, OP.add)
        dr()
        vector.tensor_tensor(out_rs12, out_rs, sl(A_MU), OP.mult)
        return vector.tensor_tensor(out_nmr12, out_negmurs, sl(A_MU), OP.mult)

    def wchunk_src(k):
        if k < 32:
            dc, colblk = k % 8, k // 8
            return W1Td[128 * dc:128 * dc + 128,
                        1024 * colblk:1024 * colblk + 1024]
        j = k - 32
        return W2Td[128 * j:128 * j + 128, :]

    X4view = Xd.rearrange("(a b) d -> a b d", b=1024)[:, 0, :]

    with nc.Block() as blk:
        @blk.sync
        def _(sync):
            sync.dma_start(out=xbuf[0:4, 0, :], in_=X4view).then_inc(S["d_pa"], 16)
            for c in range(16):
                dc, h = c // 2, c % 2
                if c == 0:
                    w(sync, "p_bce", 8)
                w(sync, "p_kc", c - 1)
                sync.dma_start(out=big0[:, h * 2048:(h + 1) * 2048],
                               in_=KTd[dc * 128:(dc + 1) * 128, h * 2048:(h + 1) * 2048]
                               ).then_inc(S[("d_ka", "d_kb", "d_kc")[c % 3]], 16)
            for i in range(NT if stage >= 2 else 0):
                if i == 0:
                    w(sync, "d_cat", 64)
                w(sync, "a_xa", i - 1)
                w(sync, "d_x", 16 * i)
                sync.dma_start(out=xbuf[:, i % 2, :], in_=Xd[128 * i:128 * (i + 1), :]
                               ).then_inc(S["d_x"], 16)
            for c in range(64 if stage >= 3 else 0):
                if qname(c) != "d_pb":
                    continue
                if c == 0:
                    w(sync, "a_pe", NT - 1)
                if c == 5:
                    w(sync, "a_pv", 4 * NT)
                w(sync, "b_qv", c - 5)
                sync.dma_start(out=qslot(c), in_=wchunk_src(c)
                               ).then_inc(S[qsem(c)[0]], 16)
            for i in range(NT if stage >= 4 else 0):
                w(sync, "d_o", 16 * (i - 2))
                w(sync, "d_x", 16 * (NT + i))
                sync.dma_start(out=xbuf[:, i % 3, :], in_=Xd[128 * i:128 * (i + 1), :]
                               ).then_inc(S["d_x"], 16)

        @blk.gpsimd
        def _(gps):
            gps.iota(ident[:], [[1, 128]], channel_multiplier=-1,
                     allow_small_or_imprecise_dtypes=True).then_inc(S["g_init"], 1)
            gps.memset(onesm[:, 0:1], 1.0).then_inc(S["g_init"], 1)
            gps.memset(onesr[:], 1.0).then_inc(S["g_init"], 1)
            gps.memset(onesm[:, 1:2], MAGIC).then_inc(S["g_init"], 1)
            w(gps, "p_x4", 1)
            for j in range(4):
                gps.dma_start(out=big0[0:1, j * 1024:(j + 1) * 1024],
                              in_=xbuf[j:j + 1, 0, 0:1024]).then_inc(S["d_cat"], 16)
            for kc in range(32):
                w(gps, ("d_v", "d_va", "d_vb")[kc % 3], 16 * (kc // 3))
                gps.dma_start(out=VW2[:, kc, :], in_=Vd[128 * kc:128 * kc + 128, :]
                              ).then_inc(S[("d_v", "d_va", "d_vb")[kc % 3]], 16)
            for i in range(NT if stage >= 2 else 0):
                for k in (2 * i, 2 * i + 1):
                    w(gps, "a_wr", k)
                    gps.dma_start(out=absg[:, k % 2, :], in_=wchunk_src(k)
                                  ).then_inc(S["d_wa"], 16)
            if stage >= 3:
                for c in range(4, 64, 5):
                    if c == 4:
                        w(gps, "a_pv", 4 * NT)
                    w(gps, "b_qv", c - 4)
                    gps.dma_start(out=qslot(c), in_=wchunk_src(c)
                                  ).then_inc(S[qsem(c)[0]], 16)
            if stage < 4:
                if stage >= 2:
                    gps.wait_ge(S["d_s"], 16 * NT)
                return
            w(gps, "a_pe", NT)
            w(gps, "a_vd", NT)
            w(gps, "a_pv", 4 * NT)
            w(gps, "a_wr", 64)
            w(gps, "d_s", 16 * NT)
            gps.dma_start(out=bc16[:], in_=B1d).then_inc(S["d_bc"], 16)
            w(gps, "b_qv", 60)
            gps.dma_start(out=gbc[:], in_=Gd).then_inc(S["d_bc"], 16)
            gps.dma_start(out=bbc[:], in_=Bd).then_inc(S["d_bc"], 16)
            gps.dma_start(out=b2bc[:], in_=B2d).then_inc(S["d_bc"], 16)
            for i in range(NT):
                w(gps, "b_fin", i - 1)
                w(gps, "d_sr", 16 * i)
                gps.dma_start(out=sgbf[:, i % 2, :], in_=OUTd[128 * i:128 * (i + 1), :]
                              ).then_inc(S["d_sr"], 16)
                if i >= 1:
                    w(gps, "b_fin", i)
                    w(gps, "d_o", 16 * (i - 1))
                    gps.dma_start(out=OUTd[128 * (i - 1):128 * i, :],
                                  in_=xbuf[:, (i - 1) % 3, :]).then_inc(S["d_o"], 16)
            w(gps, "b_fin", NT)
            w(gps, "d_o", 16 * (NT - 1))
            gps.dma_start(out=OUTd[128 * (NT - 1):128 * NT, :],
                          in_=xbuf[:, (NT - 1) % 3, :]).then_inc(S["d_o"], 16)
            gps.wait_ge(S["d_o"], 16 * NT)

        @blk.vector
        def _(vector):
            w(vector, "g_init", 4)
            vector.tensor_scalar(ident[:], ident[:], 0.0, None, OP.is_equal
                                 ).then_inc(S["v_init"], 1)
            w(vector, "d_pa", 16)
            vector.tensor_scalar(scrb[0:4, :], xbuf[0:4, 0, :], 0.0, 0.0, OP.add, OP.add,
                                 accum_out=stA[0:4, 0, A_SUM:A_SUM + 1]).then_inc(S["p_s4a"], 1)
            w(vector, "p_sq4", 1)
            newton_rsqrt(vector, stA, 0, stA[0:4, 0, A_Y:A_Y + 1],
                         stA[0:4, 0, A_NM:A_NM + 1], npart=4).then_inc(S["p_ln4"], 1)
            for g in range(8):
                w(vector, "p_bc", g + 1)
                vector.tensor_copy(bc16[:, g * 512:(g + 1) * 512], psA[:, g % 2, :]
                                   ).then_inc(S["p_bce"], 1)
            for c in range(16):
                dc, h = c // 2, c % 2
                w(vector, ("d_ka", "d_kb", "d_kc")[c % 3], 16 * (c // 3 + 1))
                vector.tensor_tensor(KW1[:, dc, h * 2048:(h + 1) * 2048],
                                     big0[:, h * 2048:(h + 1) * 2048],
                                     bc16[:, h * 2048:(h + 1) * 2048], OP.mult
                                     ).then_inc(S["p_kc"], 1)
            if stage >= 2:
                w(vector, "d_x", 16)
                vector.tensor_scalar(scrb[:], xbuf[:, 0, :], 0.0, 0.0, OP.add, OP.add,
                                     accum_out=stA[:, 0, A_SUM:A_SUM + 1])
                w(vector, "a_sq", 1)
                newton_rsqrt(vector, stA, 0, stP[:, 0, 0:1], stP[:, 0, 1:2],
                             out_rs12=stP[:, 0, 2:3], out_nmr12=stP[:, 0, 3:4]
                             ).then_inc(S["a_ln"], 1)
            for k in range(NT if stage >= 2 else 0):
                par = k % 2
                w(vector, "a_xt", k + 1)
                vector.tensor_copy(xnT[:], psX[:]).then_inc(S["a_xe"], 1)
                if k >= 2:
                    w(vector, "a_vd", k - 1)
                    w(vector, "d_s", 16 * (k - 3))
                    vector.tensor_scalar(sgbuf[:, k % 2, 0:512], psV[:, 0:512],
                                         stA[:, k % 2, A_RL:A_RL + 1], None, OP.mult)
                    vector.tensor_scalar(sgbuf[:, k % 2, 512:1024], psV[:, 512:1024],
                                         stA[:, k % 2, A_RL:A_RL + 1], None, OP.mult
                                         ).then_inc(S["a_ve"], 1)
                for kg in range(8):
                    if kg == 0:
                        w(vector, "a_pe", k - 1)
                    w(vector, "a_sd", 8 * k + kg + 1)
                    vector.tensor_scalar(SS[par][:, kg * 512:(kg + 1) * 512],
                                         psA[:, kg % 2, :], 0.0, NEG, OP.add, OP.max,
                                         accum_out=mpt[:, kg:kg + 1]
                                         ).then_inc(S["a_se"], 1)
                if k >= 1:
                    w(vector, "a_pe", k)
                    vector.reciprocal(stA[:, 1 - par, A_RL:A_RL + 1],
                                      stA[:, 1 - par, A_L:A_L + 1]).then_inc(S["a_rl"], 1)
                    vector.drain()
                    for tb in range(4):
                        w(vector, "a_pt", 4 * (k - 1) + tb + 1)
                        vector.tensor_copy(PT8[:, tb * 1024:(tb + 1) * 1024],
                                           psT[:, tb % 2, :]).then_inc(S["a_pv"], 1)
                vector.drain()
                vector.reduce_max(stA[:, par, A_NM:A_NM + 1], mpt[:], axis=AX,
                                  negate=True).then_inc(S["a_me"], 1)
                if k + 1 < NT:
                    w(vector, "d_x", 16 * (k + 2))
                    vector.tensor_scalar(scrb[:], xbuf[:, (k + 1) % 2, :], 0.0, 0.0,
                                         OP.add, OP.add,
                                         accum_out=stA[:, (k + 1) % 2, A_SUM:A_SUM + 1])
                    w(vector, "a_sq", k + 2)
                    newton_rsqrt(vector, stA, (k + 1) % 2, stP[:, k + 1, 0:1],
                                 stP[:, k + 1, 1:2], out_rs12=stP[:, k + 1, 2:3],
                                 out_nmr12=stP[:, k + 1, 3:4]).then_inc(S["a_ln"], 1)
            if stage >= 2:
                par = (NT - 1) % 2
                w(vector, "a_pe", NT)
                vector.reciprocal(stA[:, par, A_RL:A_RL + 1], stA[:, par, A_L:A_L + 1]
                                  ).then_inc(S["a_rl"], 1)
                vector.drain()
                for tb in range(4):
                    w(vector, "a_pt", 4 * (NT - 1) + tb + 1)
                    vector.tensor_copy(PT8[:, tb * 1024:(tb + 1) * 1024], psT[:, tb % 2, :]
                                       ).then_inc(S["a_pv"], 1)
                for j in (NT - 2, NT - 1):
                    w(vector, "a_vd", j + 1)
                    w(vector, "d_s", 16 * (j - 1))
                    vector.tensor_scalar(sgbuf[:, j % 2, 0:512], psV[:, 0:512],
                                         stA[:, j % 2, A_RL:A_RL + 1], None, OP.mult)
                    vector.tensor_scalar(sgbuf[:, j % 2, 512:1024], psV[:, 512:1024],
                                         stA[:, j % 2, A_RL:A_RL + 1], None, OP.mult
                                         ).then_inc(S["a_ve"], 1)
            if stage < 3:
                return
            w(vector, "a_wr", 64)
            vector.reduce_sum(accw[:, 0:1], wpart[:, 0:32], axis=AX)
            vector.reduce_sum(accw[:, 1:2], wpart[:, 32:64], axis=AX
                              ).then_inc(S["t_vw"], 1)
            w(vector, "t_pr", 2)
            vector.tensor_copy(wtmp[0:1, 0:2], psR[0:1, 0:2])
            vector.drain()
            vector.tensor_scalar(wtmp[0:1, 2:4], wtmp[0:1, 0:2], 1.0 / (D * H), EPS,
                                 OP.mult, OP.max).then_inc(S["t_vw"], 1)
            w(vector, "t_pb", 2)
            vector.tensor_copy(mwbc, psR[:, 2:4])
            vector.drain()
            vector.reciprocal(swbc, mwbc).then_inc(S["t_sw"], 1)
            vector.drain()
            for c in range(66):
                if c < 64:
                    col = 0 if c < 32 else 1
                    sem_n, cnt = qsem(c)
                    w(vector, sem_n, cnt)
                    src = qslot(c)
                    vector.tensor_scalar(src, src, swbc[:, col:col + 1],
                                         MAGIC, OP.mult, OP.add).then_inc(S["b_q1"], 1)
                if c >= 2:
                    cc = c - 2
                    w(vector, "b_q2", cc + 1)
                    dest = (KW1[:, cc % 8, (cc // 8) * 1024:(cc // 8 + 1) * 1024] if cc < 32
                            else VW2[:, cc - 32, :])
                    vector.tensor_scalar(dest, qslot(cc), -1.0, 1.0, OP.max, OP.min
                                         ).then_inc(S["b_qv"], 1)
            if stage < 4:
                return
            w(vector, "d_s", 16 * NT)
            w(vector, "a_wr", 64)
            w(vector, "a_pv", 4 * NT)
            w(vector, "d_bc", 16 * 4)
            w(vector, "d_wa", 16 * 64)

            def x2evac(im_, hb):
                w(vector, "b_pt", 8 * im_ + hb + 1)
                vector.tensor_copy(x2T[:, hb % 2, :], psT[:, hb % 2, 0:512]
                                   ).then_inc(S["b_pv"], 1)

            def q1_chain(j, ev=None):
                pj = j % 2
                sj = lambda kk: stB[:, pj, kk:kk + 1]

                def step():
                    if ev is not None:
                        hb = next(ev[1], None)
                        if hb is not None:
                            x2evac(ev[0], hb)
                w(vector, "d_x", 16 * (NT + j + 1))
                w(vector, "b_xq", j)
                vector.tensor_scalar(xn_, xbuf[:, j % 3, :], stP[:, j, 2:3],
                                     stP[:, j, 3:4], OP.mult, OP.add)
                step()
                vector.tensor_tensor(xn_, xn_, gbc[:], OP.mult)
                step()
                vector.tensor_tensor(xn_, xn_, bbc[:], OP.add)
                step()
                vector.reduce_max(sj(B_AM1), xn_, axis=AX, apply_absolute_value=True)
                step()
                vector.drain()
                vector.tensor_scalar(sj(B_M1), sj(B_AM1), EPS, None, OP.max)
                vector.drain()
                vector.reciprocal(sj(B_T), sj(B_M1))
                vector.drain()
                vector.tensor_scalar(sj(B_SX1), sj(B_T), 127.0, None, OP.mult)
                vector.tensor_tensor(sj(B_T), sj(B_M1), mwbc[:, 0:1], OP.mult)
                vector.drain()
                vector.tensor_scalar(sj(B_DS1), sj(B_T), 1.0 / 127.0, None, OP.mult)
                vector.drain()
                step()
                vector.tensor_scalar(xn_, xn_, sj(B_SX1), MAGIC, OP.mult, OP.add
                                     ).then_inc(S["b_q1a"], 1)

            q1_chain(0)
            w(vector, "b_xt", 1)
            vector.tensor_copy(xnT[:], psX[:]).then_inc(S["b_xe"], 1)
            for i in range(NT):
                par = i % 2
                sB = lambda kk: stB[:, par, kk:kk + 1]
                sBp = lambda kk: stB[:, 1 - par, kk:kk + 1]
                evs = list(range(8) if i == 0 else range(2, 8))
                ams = list(range(8))
                order = []
                while evs or ams:
                    if evs:
                        order.append(("e", evs.pop(0)))
                    if ams:
                        order.append(("a", ams.pop(0)))
                if i >= 1:
                    ai = [n for n, kg in enumerate(order) if kg == ("a", 6)]
                    order.insert(ai[0], ("h", 0))
                    aj = [n for n, kg in enumerate(order) if kg == ("a", 7)]
                    order.insert(aj[0], ("h", 1))
                for kind, g in order:
                    if kind == "h":
                        x2evac(i - 1, g)
                    elif kind == "e":
                        w(vector, "b_fd", 8 * i + g + 1)
                        if g == 0:
                            w(vector, "b_q2c", i)
                        vector.scalar_tensor_tensor(big0[:, g * 512:(g + 1) * 512],
                                                    psA[:, g % 2, :], sB(B_DS1),
                                                    bc16[:, g * 512:(g + 1) * 512],
                                                    OP.mult, OP.add).then_inc(S["b_fe"], 1)
                    else:
                        w(vector, "b_ge", 8 * i + g + 1)
                        vector.reduce_max(mpt[:, g:g + 1],
                                          big0[:, g * 512:(g + 1) * 512], axis=AX,
                                          apply_absolute_value=True)
                vector.drain()
                vector.reduce_max(sB(B_AM2), mpt[:], axis=AX)
                vector.drain()
                vector.tensor_scalar(sB(B_M2), sB(B_AM2), EPS, None, OP.max)
                vector.drain()
                vector.reciprocal(sB(B_T), sB(B_M2))
                vector.drain()
                vector.tensor_scalar(sB(B_SX2), sB(B_T), 127.0, None, OP.mult)
                vector.tensor_tensor(sB(B_T), sB(B_M2), mwbc[:, 1:2], OP.mult)
                vector.drain()
                vector.tensor_scalar(sB(B_DS2), sB(B_T), 1.0 / 127.0, None, OP.mult
                                     ).then_inc(S["b_q2s"], 1)
                vector.drain()
                it = iter(range(2, 8))
                if i + 1 < NT:
                    q1_chain(i + 1, ev=(i - 1, it) if i >= 1 else None)
                if i >= 1:
                    for hb in it:
                        x2evac(i - 1, hb)
                if i + 1 < NT:
                    w(vector, "b_xt", i + 2)
                    vector.tensor_copy(xnT[:], psX[:]).then_inc(S["b_xe"], 1)
                if i >= 1:
                    w(vector, "b_vd", i)
                    vector.scalar_tensor_tensor(gt_[:, 0:512], psV[:, 0:512], sBp(B_DS2),
                                                b2bc[:, 0:512], OP.mult, OP.add)
                    vector.scalar_tensor_tensor(gt_[:, 512:1024], psV[:, 512:1024],
                                                sBp(B_DS2), b2bc[:, 512:1024],
                                                OP.mult, OP.add).then_inc(S["b_y2"], 1)
                if i + 1 < NT:
                    sN = lambda kk: stB[:, (i + 1) % 2, kk:kk + 1]
                    for hg in range(2):
                        if hg == 0:
                            w(vector, "b_q2c", i + 1)
                        w(vector, "b_fd", 8 * (i + 1) + hg + 1)
                        vector.scalar_tensor_tensor(big0[:, hg * 512:(hg + 1) * 512],
                                                    psA[:, hg % 2, :], sN(B_DS1),
                                                    bc16[:, hg * 512:(hg + 1) * 512],
                                                    OP.mult, OP.add).then_inc(S["b_fe"], 1)
                if i >= 1:
                    w(vector, "b_sl", i)
                    w(vector, "d_sr", 16 * i)
                    vector.tensor_tensor(gt_, gt_, sgbf[:, (i - 1) % 2, :], OP.mult)
                    vector.tensor_tensor(xbuf[:, (i - 1) % 3, :], xbuf[:, (i - 1) % 3, :],
                                         gt_, OP.add).then_inc(S["b_fin"], 1)
            par = (NT - 1) % 2
            sB = lambda kk: stB[:, par, kk:kk + 1]
            for hb in range(8):
                w(vector, "b_pt", 8 * (NT - 1) + hb + 1)
                vector.tensor_copy(x2T[:, hb % 2, :], psT[:, hb % 2, 0:512]
                                   ).then_inc(S["b_pv"], 1)
            w(vector, "b_vd", NT)
            vector.scalar_tensor_tensor(gt_[:, 0:512], psV[:, 0:512], sB(B_DS2),
                                        b2bc[:, 0:512], OP.mult, OP.add)
            vector.scalar_tensor_tensor(gt_[:, 512:1024], psV[:, 512:1024], sB(B_DS2),
                                        b2bc[:, 512:1024], OP.mult, OP.add
                                        ).then_inc(S["b_y2"], 1)
            w(vector, "b_sl", NT)
            w(vector, "d_sr", 16 * NT)
            vector.tensor_tensor(gt_, gt_, sgbf[:, (NT - 1) % 2, :], OP.mult)
            vector.tensor_tensor(xbuf[:, (NT - 1) % 3, :], xbuf[:, (NT - 1) % 3, :],
                                 gt_, OP.add).then_inc(S["b_fin"], 1)

        @blk.scalar
        def _(act):
            w(act, "p_s4a", 1)
            act.activation(scrb[0:4, :], xbuf[0:4, 0, :], AF.Square,
                           accum_out=stA[0:4, 0, A_SSQ:A_SSQ + 1]).then_inc(S["p_sq4"], 1)
            w(act, "p_ln4", 1)
            act.activation(xbuf[0:4, 0, :], xbuf[0:4, 0, :], AF.Identity,
                           bias=stA[0:4, 0, A_NM:A_NM + 1],
                           scale=stA[0:4, 0, A_Y:A_Y + 1]).then_inc(S["p_x4"], 1)
            if stage >= 2:
                w(act, "d_x", 16)
                act.activation(scrb[:], xbuf[:, 0, :], AF.Square,
                               accum_out=stA[:, 0, A_SSQ:A_SSQ + 1]).then_inc(S["a_sq"], 1)
                w(act, "a_ln", 1)
                act.activation(xnb[:], xbuf[:, 0, :], AF.Identity,
                               bias=stP[:, 0, 1:2], scale=stP[:, 0, 0:1]
                               ).then_inc(S["a_xa"], 1)
            for k in range(NT if stage >= 2 else 0):
                par = k % 2
                if k >= 1:
                    w(act, "a_me", k)
                    w(act, "a_pt", 4 * (k - 1))
                    act.activation(P8[:], SS[1 - par][:], AF.Exp,
                                   bias=stA[:, 1 - par, A_NM:A_NM + 1],
                                   accum_out=stA[:, 1 - par, A_L:A_L + 1]
                                   ).then_inc(S["a_pe"], 1)
                if k + 1 < NT:
                    w(act, "d_x", 16 * (k + 2))
                    act.activation(scrb[:], xbuf[:, (k + 1) % 2, :], AF.Square,
                                   accum_out=stA[:, (k + 1) % 2, A_SSQ:A_SSQ + 1]
                                   ).then_inc(S["a_sq"], 1)
                    w(act, "a_ln", k + 2)
                    w(act, "a_xt", k + 1)
                    act.activation(xnb[:], xbuf[:, (k + 1) % 2, :], AF.Identity,
                                   bias=stP[:, k + 1, 1:2], scale=stP[:, k + 1, 0:1]
                                   ).then_inc(S["a_xa"], 1)
                for kk in (2 * k, 2 * k + 1):
                    w(act, "d_wa", 16 * (kk + 1))
                    act.activation(scrb[:], absg[:, kk % 2, :], AF.Abs,
                                   accum_out=wpart[:, kk:kk + 1]).then_inc(S["a_wr"], 1)
                if k >= 2:
                    w(act, "a_ve", k - 1)
                    act.dma_start(out=OUTd[128 * (k - 2):128 * (k - 1), :],
                                  in_=sgbuf[:, k % 2, :]).then_inc(S["d_s"], 16)
            if stage >= 2:
                par = (NT - 1) % 2
                w(act, "a_me", NT)
                w(act, "a_pt", 4 * (NT - 1))
                act.activation(P8[:], SS[par][:], AF.Exp, bias=stA[:, par, A_NM:A_NM + 1],
                               accum_out=stA[:, par, A_L:A_L + 1]).then_inc(S["a_pe"], 1)
                for j in (NT - 2, NT - 1):
                    w(act, "a_ve", j + 1)
                    act.dma_start(out=OUTd[128 * j:128 * (j + 1), :],
                                  in_=sgbuf[:, j % 2, :]).then_inc(S["d_s"], 16)
            if stage >= 3:
                act.dma_start(out=qslot(1), in_=wchunk_src(1)
                              ).then_inc(S[qsem(1)[0]], 16)
                act.dma_start(out=qslot(3), in_=wchunk_src(3)
                              ).then_inc(S[qsem(3)[0]], 16)
            for j in range(64 if stage >= 3 else 0):
                w(act, "b_q1", j + 1)
                act.activation(qslot(j), qslot(j), AF.Copy, bias=-MAGIC
                               ).then_inc(S["b_q2"], 1)
                c = j + 6
                if c < 64 and qname(c) == "d_pb3":
                    if c == 11:
                        w(act, "a_pv", 4 * NT)
                    w(act, "b_qv", c - 5)
                    act.dma_start(out=qslot(c), in_=wchunk_src(c)
                                  ).then_inc(S[qsem(c)[0]], 16)
            if stage < 4:
                return
            w(act, "b_q1a", 1)
            act.activation(xnb[:], xn_, AF.Copy, bias=-MAGIC).then_inc(S["b_xq"], 1)
            for i in range(NT):
                par = i % 2
                for hg in range(8):
                    w(act, "b_fe", 8 * i + hg + 1)
                    act.activation(big0[:, hg * 512:(hg + 1) * 512],
                                   big0[:, hg * 512:(hg + 1) * 512], AF.Gelu
                                   ).then_inc(S["b_ge"], 1)
                w(act, "b_q2s", i + 1)
                act.activation(big0[:], big0[:], AF.Identity,
                               bias=magicc, scale=stB[:, par, B_SX2:B_SX2 + 1])
                if i + 1 < NT:
                    w(act, "b_q1a", i + 2)
                    w(act, "b_xt", i + 1)
                    act.activation(xnb[:], xn_, AF.Copy, bias=-MAGIC).then_inc(S["b_xq"], 1)
                w(act, "b_pt", 8 * i)
                act.activation(x2q[:, :, :], big0[:], AF.Copy, bias=-MAGIC
                               ).then_inc(S["b_q2c"], 1)
                if i >= 1:
                    w(act, "b_y2", i)
                    act.activation(gt_, gt_, AF.Silu).then_inc(S["b_sl"], 1)
            w(act, "b_y2", NT)
            act.activation(gt_, gt_, AF.Silu).then_inc(S["b_sl"], 1)

        @blk.tensor
        def _(pe):
            w(pe, "g_init", 4)
            w(pe, "d_cat", 64)
            for g in range(8):
                w(pe, "p_bce", g - 1)
                pe.matmul(psA[:, g % 2, :], onesr[0:1, 0:128],
                          big0[0:1, g * 512:(g + 1) * 512], start=True, stop=True
                          ).then_inc(S["p_bc"], 1)
            for k in range(NT if stage >= 2 else 0):
                par = k % 2
                w(pe, "a_xa", k + 1)
                w(pe, "a_xe", k)
                if k == 0:
                    w(pe, "v_init", 1)
                for dc in range(8):
                    last = pe.transpose(psX[:, dc * 128:(dc + 1) * 128],
                                        xnb[:, dc * 128:(dc + 1) * 128], ident[:])
                last.then_inc(S["a_xt"], 1)
                for kg in range(8):
                    if kg == 0:
                        if k == 0:
                            w(pe, "p_bce", 8)
                            w(pe, "p_kc", 16)
                        w(pe, "a_xe", k + 1)
                    w(pe, "a_se", 8 * k + kg - 1)
                    for dc in range(8):
                        last = pe.matmul(psA[:, kg % 2, :], xnT[:, dc * 128:(dc + 1) * 128],
                                         KW1[:, dc, kg * 512:(kg + 1) * 512],
                                         start=(dc == 0), stop=(dc == 7))
                    last.then_inc(S["a_sd"], 1)
                if k >= 1:
                    km = k - 1
                    for tb in range(4):
                        if tb == 0:
                            w(pe, "a_pe", k)
                        w(pe, "a_pv", 4 * km + tb - 1)
                        for tc in range(8):
                            kc = 8 * tb + tc
                            last = pe.transpose(psT[:, tb % 2, tc * 128:(tc + 1) * 128],
                                                P8[:, kc * 128:(kc + 1) * 128], ident[:])
                        last.then_inc(S["a_pt"], 1)
                    w(pe, "a_ve", km)
                    for kc in range(32):
                        if km == 0:
                            w(pe, ("d_v", "d_va", "d_vb")[kc % 3], 16 * (kc // 3 + 1))
                        if kc % 8 == 0:
                            w(pe, "a_pv", 4 * km + kc // 8 + 1)
                        pe.matmul(psV[:, 0:512], PT8[:, kc * 128:(kc + 1) * 128],
                                  VW2[:, kc, 0:512], start=(kc == 0), stop=(kc == 31))
                        last = pe.matmul(psV[:, 512:1024], PT8[:, kc * 128:(kc + 1) * 128],
                                         VW2[:, kc, 512:1024], start=(kc == 0),
                                         stop=(kc == 31))
                    last.then_inc(S["a_vd"], 1)
            if stage >= 2:
                km = NT - 1
                for tb in range(4):
                    if tb == 0:
                        w(pe, "a_pe", NT)
                    w(pe, "a_pv", 4 * km + tb - 1)
                    for tc in range(8):
                        kc = 8 * tb + tc
                        last = pe.transpose(psT[:, tb % 2, tc * 128:(tc + 1) * 128],
                                            P8[:, kc * 128:(kc + 1) * 128], ident[:])
                    last.then_inc(S["a_pt"], 1)
                w(pe, "a_ve", km)
                for kc in range(32):
                    if kc % 8 == 0:
                        w(pe, "a_pv", 4 * km + kc // 8 + 1)
                    pe.matmul(psV[:, 0:512], PT8[:, kc * 128:(kc + 1) * 128],
                              VW2[:, kc, 0:512], start=(kc == 0), stop=(kc == 31))
                    last = pe.matmul(psV[:, 512:1024], PT8[:, kc * 128:(kc + 1) * 128],
                                     VW2[:, kc, 512:1024], start=(kc == 0), stop=(kc == 31))
                last.then_inc(S["a_vd"], 1)
            if stage < 3:
                return
            w(pe, "t_vw", 1)
            pe.matmul(psR[0:1, 0:1], accw[:, 0:1], ones, start=True, stop=True
                      ).then_inc(S["t_pr"], 1)
            pe.matmul(psR[0:1, 1:2], accw[:, 1:2], ones, start=True, stop=True
                      ).then_inc(S["t_pr"], 1)
            w(pe, "t_vw", 2)
            pe.matmul(psR[:, 2:3], onesr[0:1, :], wtmp[0:1, 2:3], start=True, stop=True
                      ).then_inc(S["t_pb"], 1)
            pe.matmul(psR[:, 3:4], onesr[0:1, :], wtmp[0:1, 3:4], start=True, stop=True
                      ).then_inc(S["t_pb"], 1)
            if stage < 4:
                return
            w(pe, "b_xq", 1)
            for dc in range(8):
                last = pe.transpose(psX[:, dc * 128:(dc + 1) * 128],
                                    xnb[:, dc * 128:(dc + 1) * 128], ident[:])
            last.then_inc(S["b_xt"], 1)
            for i in range(NT):
                par = i % 2
                for hg in range(8):
                    if i == 0:
                        w(pe, "b_qv", 8 * (hg // 2) + 8)
                    if hg == 0:
                        w(pe, "b_xe", i + 1)
                    w(pe, "b_fe", 8 * i + hg - 1)
                    for dc in range(8):
                        last = pe.matmul(psA[:, hg % 2, :], xnT[:, dc * 128:(dc + 1) * 128],
                                         KW1[:, dc, hg * 512:(hg + 1) * 512],
                                         start=(dc == 0), stop=(dc == 7))
                    last.then_inc(S["b_fd"], 1)
                if i == 0:
                    w(pe, "b_xq", 2)
                    for dc in range(8):
                        last = pe.transpose(psX[:, dc * 128:(dc + 1) * 128],
                                            xnb[:, dc * 128:(dc + 1) * 128], ident[:])
                    last.then_inc(S["b_xt"], 1)
                if i >= 1:
                    im = i - 1
                    if im == 0:
                        w(pe, "b_qv", 64)
                    w(pe, "b_y2", im)
                    for hb in range(9):
                        if hb < 8:
                            if hb == 0:
                                w(pe, "b_q2c", im + 1)
                            w(pe, "b_pv", 8 * im + hb - 1)
                            for c in range(4):
                                last = pe.transpose(psT[:, hb % 2, c * 128:(c + 1) * 128],
                                                    x2q[:, hb, c * 128:(c + 1) * 128],
                                                    ident[:])
                            last.then_inc(S["b_pt"], 1)
                        if hb == 8 and i + 1 < NT:
                            w(pe, "b_xq", i + 2)
                            for dc in range(8):
                                last = pe.transpose(psX[:, dc * 128:(dc + 1) * 128],
                                                    xnb[:, dc * 128:(dc + 1) * 128],
                                                    ident[:])
                            last.then_inc(S["b_xt"], 1)
                        if hb >= 1:
                            hq = hb - 1
                            w(pe, "b_pv", 8 * im + hq + 1)
                            for c in range(4):
                                kc = 4 * hq + c
                                pe.matmul(psV[:, 0:512], x2T[:, hq % 2, c * 128:(c + 1) * 128],
                                          VW2[:, kc, 0:512], start=(kc == 0), stop=(kc == 31))
                                last = pe.matmul(psV[:, 512:1024],
                                                 x2T[:, hq % 2, c * 128:(c + 1) * 128],
                                                 VW2[:, kc, 512:1024],
                                                 start=(kc == 0), stop=(kc == 31))
                    last.then_inc(S["b_vd"], 1)
            im = NT - 1
            w(pe, "b_y2", im)
            for hb in range(9):
                if hb < 8:
                    if hb == 0:
                        w(pe, "b_q2c", im + 1)
                    w(pe, "b_pv", 8 * im + hb - 1)
                    for c in range(4):
                        last = pe.transpose(psT[:, hb % 2, c * 128:(c + 1) * 128],
                                            x2q[:, hb, c * 128:(c + 1) * 128], ident[:])
                    last.then_inc(S["b_pt"], 1)
                if hb >= 1:
                    hq = hb - 1
                    w(pe, "b_pv", 8 * im + hq + 1)
                    for c in range(4):
                        kc = 4 * hq + c
                        pe.matmul(psV[:, 0:512], x2T[:, hq % 2, c * 128:(c + 1) * 128],
                                  VW2[:, kc, 0:512], start=(kc == 0), stop=(kc == 31))
                        last = pe.matmul(psV[:, 512:1024],
                                         x2T[:, hq % 2, c * 128:(c + 1) * 128],
                                         VW2[:, kc, 512:1024], start=(kc == 0),
                                         stop=(kc == 31))
            last.then_inc(S["b_vd"], 1)

    return nc


_NC_CACHE = []


def kernel(L0_attn_q, L0_attn_k, L0_attn_v, L0_attn_scale_factor, L0_attn_dropout_p,
           L0_ffn_w1, L0_ffn_b1, L0_ffn_w2, L0_ffn_b2, L0_ffn_gamma, L0_ffn_beta):
    q = np.ascontiguousarray(np.asarray(L0_attn_q))
    kT = np.ascontiguousarray(np.asarray(L0_attn_k)[0].T)
    v0 = np.ascontiguousarray(np.asarray(L0_attn_v)[0])
    w1T = np.ascontiguousarray(np.asarray(L0_ffn_w1).T)
    w2T = np.ascontiguousarray(np.asarray(L0_ffn_w2).T)
    b1bc = np.ascontiguousarray(np.broadcast_to(np.asarray(L0_ffn_b1)[None, :], (128, H)))
    b2bc = np.ascontiguousarray(np.broadcast_to(np.asarray(L0_ffn_b2)[None, :], (128, D)))
    gbc = np.ascontiguousarray(np.broadcast_to(np.asarray(L0_ffn_gamma)[None, :], (128, D)))
    bbc = np.ascontiguousarray(np.broadcast_to(np.asarray(L0_ffn_beta)[None, :], (128, D)))

    if not _NC_CACHE:
        _NC_CACHE.append(build_program())
    nc = _NC_CACHE[0]
    in_maps = [dict(x=np.ascontiguousarray(q[c]), kT=kT, v=v0, w1T=w1T, w2T=w2T,
                    b1bc=b1bc, b2bc=b2bc, gbc=gbc, bbc=bbc) for c in range(8)]
    res = run_bass_kernel_spmd(nc, in_maps, core_ids=list(range(8)))
    return np.stack([np.asarray(res.results[c]["out"]) for c in range(8)], axis=0)


# version 23
# speedup vs baseline: 1.0063x; 1.0063x over previous
import sys
if "/opt/trn_rl_repo" not in sys.path:
    sys.path.insert(0, "/opt/trn_rl_repo")
import numpy as np
import concourse.bass as bass
import concourse.mybir as mybir
from concourse.bass_utils import run_bass_kernel_spmd
from concourse.alu_op_type import AluOpType as OP

F32 = mybir.dt.float32
BF16 = mybir.dt.bfloat16
AF = mybir.ActivationFunctionType
AX = mybir.AxisListType.X

MAGIC = 1.5 * 2.0 ** 23
EPS = 1e-5
NT = 32
D = 1024
SK = 4096
H = 4096
NEG = -3.0e38


def build_program(stage=4):
    nc = bass.Bass("TRN2", target_bir_lowering=False, detect_race_conditions=False)

    Xd = nc.dram_tensor("x", [SK, D], F32, kind="ExternalInput").ap()
    KTd = nc.dram_tensor("kT", [D, SK], F32, kind="ExternalInput").ap()
    Vd = nc.dram_tensor("v", [SK, D], F32, kind="ExternalInput").ap()
    W1Td = nc.dram_tensor("w1T", [D, H], F32, kind="ExternalInput").ap()
    W2Td = nc.dram_tensor("w2T", [H, D], F32, kind="ExternalInput").ap()
    B1d = nc.dram_tensor("b1bc", [128, H], F32, kind="ExternalInput").ap()
    B2d = nc.dram_tensor("b2bc", [128, D], F32, kind="ExternalInput").ap()
    Gd = nc.dram_tensor("gbc", [128, D], F32, kind="ExternalInput").ap()
    Bd = nc.dram_tensor("bbc", [128, D], F32, kind="ExternalInput").ap()
    OUTd = nc.dram_tensor("out", [SK, D], F32, kind="ExternalOutput").ap()

    cur = [(nc.sbuf_base + 63) // 64 * 64]
    offs = {}

    def sb(name, shape, dt, alias_at=None):
        nbytes = int(np.prod(shape[1:])) * (2 if dt == BF16 else 4)
        off = cur[0] if alias_at is None else alias_at
        t = nc.alloc_sbuf_tensor_at(name, shape, dt, offset=off).ap()
        if alias_at is None:
            cur[0] += (nbytes + 31) // 32 * 32
        offs[name] = off
        return t

    KW1 = sb("KW1", [128, 8, SK], BF16)
    VW2 = sb("VW2", [128, 32, D], BF16)
    big0 = sb("big0", [128, SK], F32)
    bc16 = sb("bc16", [128, SK], F32)
    P8 = sb("P8", [128, SK], BF16)
    gbc = sb("gbc_s", [128, D], F32, alias_at=offs["P8"])
    bbc = sb("bbc_s", [128, D], F32, alias_at=offs["P8"] + 4096)
    stg45 = sb("stg45", [128, 2, 1024], F32, alias_at=offs["P8"])

    def kstg(k):
        s = k % 4
        return stg45[:, s, :] if s < 2 else stgPT[:, s - 2, :]

    def ksem(k):
        p = k // 2
        sems = ("d_ka", "d_kb", "d_kc") if k % 2 == 0 else ("d_kd", "d_ke", "d_kf")
        return sems[p % 3], 16 * (p // 3 + 1)

    def qslot(c):
        s = c % 6
        return big0[:, s * 1024:(s + 1) * 1024] if s < 4 else stg45[:, s - 4, :]

    QPAT = ("d_pb", "d_pb3", "d_pb", "d_pb3", "d_pb2")

    def qname(c):
        return QPAT[c % 5]

    def qpos(c):
        return sum(1 for j in range(c + 1) if QPAT[j % 5] == QPAT[c % 5])

    QSEMS = {"d_pb": ("d_pb", "d_pba", "d_pbb"),
             "d_pb3": ("d_pb3", "d_p3a", "d_p3b"),
             "d_pb2": ("d_pb2",)}

    def qsem(c):
        q = qname(c)
        p = qpos(c) - 1
        sems = QSEMS[q]
        return sems[p % len(sems)], 16 * (p // len(sems) + 1)
    PT8 = sb("PT8", [128, SK], BF16)
    x2q = sb("x2q", [128, 8, 512], BF16, alias_at=offs["PT8"])
    stgPT = sb("stgPT", [128, 2, 1024], F32, alias_at=offs["PT8"])
    xbuf = sb("xbuf", [128, 3, D], F32)
    sgbuf = sb("sgbuf", [128, 2, D], F32)
    sgbf = sb("sgbf", [128, 2, D], BF16, alias_at=offs["sgbuf"])
    x2T = sb("x2T", [128, 2, 512], BF16, alias_at=offs["sgbuf"] + 4096)
    xng = sb("xng", [128, D], F32)
    gtb = sb("gtb", [128, D], BF16, alias_at=offs["sgbuf"] + 6144)
    absg = sb("absg", [128, 2, D], BF16, alias_at=offs["xng"])
    xnb = sb("xnb", [128, D], BF16)
    xnT = sb("xnT", [128, D], BF16)
    scrb = sb("scrb", [128, D], BF16)
    b2bc = sb("b2bc_s", [128, D], BF16, alias_at=offs["scrb"])
    stP = sb("stP", [128, NT, 4], F32)
    stA = sb("stA", [128, 2, 9], F32)
    stB = sb("stB", [128, 2, 9], F32)
    mpt = sb("mpt", [128, 8], F32)
    accw = sb("accw", [128, 2], F32)
    wpart = sb("wpart", [128, 64], F32)
    wtmp = sb("wtmp", [128, 4], F32)
    msw = sb("msw", [128, 4], F32)
    mwbc = msw[:, 0:2]
    swbc = msw[:, 2:4]
    onesm = sb("onesm", [128, 2], F32)
    ones = onesm[:, 0:1]
    magicc = onesm[:, 1:2]
    onesr = sb("onesr", [1, 128], F32)
    ident = sb("ident", [128, 128], BF16)
    assert cur[0] <= 229376, f"SBUF over budget: {cur[0]}"

    xn_ = xng[:, 0:D]
    gt_ = gtb[:, 0:D]
    SS = [big0, bc16]

    psA = nc.alloc_psum_tensor("psA", [128, 2, 512], F32).ap()
    psV = nc.alloc_psum_tensor("psV", [128, 1024], F32).ap()
    psT = nc.alloc_psum_tensor("psT", [128, 2, 1024], BF16).ap()
    psX = nc.alloc_psum_tensor("psX", [128, 1024], BF16).ap()
    psR = nc.alloc_psum_tensor("psR", [128, 16], F32).ap()

    semnames = ("d_x d_pa d_v d_wa d_s d_pb d_bc d_sr d_o d_cat "
                "a_ln a_xa a_xt a_xe a_sd a_se a_me a_pe a_rl a_pt a_pv a_vd a_ve a_wr "
                "g_init v_init p_ln4 p_x4 p_bc p_bce p_kc "
                "t_pr t_vw t_pb t_sw b_q1 b_q2 b_qv "
                "b_l1a b_l2 b_h2 b_q1a b_xq b_xt b_xe b_fd b_fe b_ge b_q2s "
                "b_q2c b_pt b_pv b_vd b_y2 b_sl b_fin a_sq b_sq p_sq4 p_s4a d_pb2 "
                "d_pb3 d_pba d_pbb d_p3a d_p3b d_ka d_kb d_kc d_kd d_ke d_kf d_va d_vb").split()
    S = {n: nc.alloc_semaphore(n) for n in semnames}

    def w(eng, name, v):
        if v > 0:
            eng.wait_ge(S[name], v)

    A_SUM, A_SSQ, A_MU, A_W, A_Y, A_T, A_NM, A_L, A_RL = 0, 1, 2, 3, 4, 5, 6, 7, 8
    B_DS2, B_T, B_AM1, B_M1, B_SX1, B_DS1, B_AM2, B_M2, B_SX2 = range(9)

    def newton_rsqrt(vector, st, par, out_rs, out_negmurs, npart=128, iters=3,
                 out_rs12=None, out_nmr12=None):
        sl = lambda k: st[0:npart, par, k:k + 1]
        dr = vector.drain
        dr()
        vector.tensor_scalar(sl(A_MU), sl(A_SUM), 1.0 / D, None, OP.mult)
        vector.tensor_scalar(sl(A_W), sl(A_SSQ), 1.0 / D, None, OP.mult)
        dr()
        vector.tensor_tensor(sl(A_T), sl(A_MU), sl(A_MU), OP.mult)
        dr()
        vector.tensor_tensor(sl(A_W), sl(A_W), sl(A_T), OP.subtract)
        dr()
        vector.tensor_scalar(sl(A_W), sl(A_W), EPS, None, OP.add)
        dr()
        vector.tensor_scalar(sl(A_Y), sl(A_W), -0.5, 1.5, OP.mult, OP.add)
        dr()
        for _ in range(iters - 1):
            vector.tensor_tensor(sl(A_T), sl(A_Y), sl(A_Y), OP.mult)
            dr()
            vector.tensor_tensor(sl(A_T), sl(A_T), sl(A_W), OP.mult)
            dr()
            vector.tensor_scalar(sl(A_T), sl(A_T), -0.5, 1.5, OP.mult, OP.add)
            dr()
            vector.tensor_tensor(sl(A_Y), sl(A_Y), sl(A_T), OP.mult)
            dr()
        vector.tensor_copy(out_rs, sl(A_Y))
        vector.tensor_tensor(sl(A_T), sl(A_MU), sl(A_Y), OP.mult)
        dr()
        last = vector.tensor_scalar(out_negmurs, sl(A_T), -1.0, None, OP.mult)
        if out_rs12 is None:
            return last
        vector.reciprocal(sl(A_MU), sl(A_W))
        dr()
        vector.tensor_scalar(sl(A_MU), sl(A_MU), -EPS, 1.0, OP.mult, OP.add)
        dr()
        vector.tensor_scalar(sl(A_MU), sl(A_MU), -0.5, 1.5 - 0.5 * EPS, OP.mult, OP.add)
        dr()
        vector.tensor_tensor(out_rs12, out_rs, sl(A_MU), OP.mult)
        return vector.tensor_tensor(out_nmr12, out_negmurs, sl(A_MU), OP.mult)

    def wchunk_src(k):
        if k < 32:
            dc, colblk = k % 8, k // 8
            return W1Td[128 * dc:128 * dc + 128,
                        1024 * colblk:1024 * colblk + 1024]
        j = k - 32
        return W2Td[128 * j:128 * j + 128, :]

    X4view = Xd.rearrange("(a b) d -> a b d", b=1024)[:, 0, :]

    with nc.Block() as blk:
        @blk.sync
        def _(sync):
            sync.dma_start(out=xbuf[0:4, 0, :], in_=X4view).then_inc(S["d_pa"], 16)
            if stage >= 2:
                w(sync, "d_cat", 64)
                sync.dma_start(out=xbuf[:, 0, :], in_=Xd[0:128, :]).then_inc(S["d_x"], 16)
                w(sync, "d_x", 16)
                sync.dma_start(out=xbuf[:, 1, :], in_=Xd[128:256, :]).then_inc(S["d_x"], 16)
            for k in range(0, 32, 2):
                w(sync, "p_kc", k - 3)
                sync.dma_start(out=kstg(k),
                               in_=KTd[(k % 8) * 128:(k % 8 + 1) * 128,
                                       (k // 8) * 1024:(k // 8 + 1) * 1024]
                               ).then_inc(S[ksem(k)[0]], 16)
            for i in (range(2, NT) if stage >= 2 else []):
                w(sync, "a_xa", i - 1)
                w(sync, "d_x", 16 * i)
                sync.dma_start(out=xbuf[:, i % 2, :], in_=Xd[128 * i:128 * (i + 1), :]
                               ).then_inc(S["d_x"], 16)
            for c in range(64 if stage >= 3 else 0):
                if qname(c) != "d_pb":
                    continue
                if c == 0:
                    w(sync, "a_pe", NT - 1)
                if c == 5:
                    w(sync, "a_pv", 4 * NT)
                w(sync, "b_qv", c - 5)
                sync.dma_start(out=qslot(c), in_=wchunk_src(c)
                               ).then_inc(S[qsem(c)[0]], 16)
            for i in range(NT if stage >= 4 else 0):
                w(sync, "d_o", 16 * (i - 2))
                w(sync, "d_x", 16 * (NT + i))
                sync.dma_start(out=xbuf[:, i % 3, :], in_=Xd[128 * i:128 * (i + 1), :]
                               ).then_inc(S["d_x"], 16)

        @blk.gpsimd
        def _(gps):
            gps.iota(ident[:], [[1, 128]], channel_multiplier=-1,
                     allow_small_or_imprecise_dtypes=True).then_inc(S["g_init"], 1)
            gps.memset(onesm[:, 0:1], 1.0).then_inc(S["g_init"], 1)
            gps.memset(onesr[:], 1.0).then_inc(S["g_init"], 1)
            gps.memset(onesm[:, 1:2], MAGIC).then_inc(S["g_init"], 1)
            w(gps, "p_x4", 1)
            for j in range(4):
                gps.dma_start(out=big0[0:1, j * 1024:(j + 1) * 1024],
                              in_=xbuf[j:j + 1, 0, 0:1024]).then_inc(S["d_cat"], 16)
            for kc in range(32):
                w(gps, ("d_v", "d_va", "d_vb")[kc % 3], 16 * (kc // 3))
                gps.dma_start(out=VW2[:, kc, :], in_=Vd[128 * kc:128 * kc + 128, :]
                              ).then_inc(S[("d_v", "d_va", "d_vb")[kc % 3]], 16)
            for i in range(NT if stage >= 2 else 0):
                for k in (2 * i, 2 * i + 1):
                    w(gps, "a_wr", k)
                    gps.dma_start(out=absg[:, k % 2, :], in_=wchunk_src(k)
                                  ).then_inc(S["d_wa"], 16)
            if stage >= 3:
                for c in range(4, 64, 5):
                    if c == 4:
                        w(gps, "a_pv", 4 * NT)
                    w(gps, "b_qv", c - 4)
                    gps.dma_start(out=qslot(c), in_=wchunk_src(c)
                                  ).then_inc(S[qsem(c)[0]], 16)
            if stage < 4:
                if stage >= 2:
                    gps.wait_ge(S["d_s"], 16 * NT)
                return
            w(gps, "a_pe", NT)
            w(gps, "a_vd", NT)
            w(gps, "a_pv", 4 * NT)
            w(gps, "a_wr", 64)
            w(gps, "d_s", 16 * NT)
            gps.dma_start(out=bc16[:], in_=B1d).then_inc(S["d_bc"], 16)
            w(gps, "b_qv", 60)
            gps.dma_start(out=gbc[:], in_=Gd).then_inc(S["d_bc"], 16)
            gps.dma_start(out=bbc[:], in_=Bd).then_inc(S["d_bc"], 16)
            gps.dma_start(out=b2bc[:], in_=B2d).then_inc(S["d_bc"], 16)
            for i in range(NT):
                w(gps, "b_fin", i - 1)
                w(gps, "d_sr", 16 * i)
                gps.dma_start(out=sgbf[:, i % 2, :], in_=OUTd[128 * i:128 * (i + 1), :]
                              ).then_inc(S["d_sr"], 16)
                if i >= 1:
                    w(gps, "b_fin", i)
                    w(gps, "d_o", 16 * (i - 1))
                    gps.dma_start(out=OUTd[128 * (i - 1):128 * i, :],
                                  in_=xbuf[:, (i - 1) % 3, :]).then_inc(S["d_o"], 16)
            w(gps, "b_fin", NT)
            w(gps, "d_o", 16 * (NT - 1))
            gps.dma_start(out=OUTd[128 * (NT - 1):128 * NT, :],
                          in_=xbuf[:, (NT - 1) % 3, :]).then_inc(S["d_o"], 16)
            gps.wait_ge(S["d_o"], 16 * NT)

        @blk.vector
        def _(vector):
            w(vector, "g_init", 4)
            vector.tensor_scalar(ident[:], ident[:], 0.0, None, OP.is_equal
                                 ).then_inc(S["v_init"], 1)
            w(vector, "d_pa", 16)
            vector.tensor_scalar(scrb[0:4, :], xbuf[0:4, 0, :], 0.0, 0.0, OP.add, OP.add,
                                 accum_out=stA[0:4, 0, A_SUM:A_SUM + 1]).then_inc(S["p_s4a"], 1)
            w(vector, "p_sq4", 1)
            newton_rsqrt(vector, stA, 0, stA[0:4, 0, A_Y:A_Y + 1],
                         stA[0:4, 0, A_NM:A_NM + 1], npart=4).then_inc(S["p_ln4"], 1)
            for g in range(8):
                w(vector, "p_bc", g + 1)
                vector.tensor_copy(bc16[:, g * 512:(g + 1) * 512], psA[:, g % 2, :]
                                   ).then_inc(S["p_bce"], 1)
            if stage >= 2:
                w(vector, "d_x", 16)
                vector.tensor_scalar(scrb[:], xbuf[:, 0, :], 0.0, 0.0, OP.add, OP.add,
                                     accum_out=stA[:, 0, A_SUM:A_SUM + 1])
                w(vector, "a_sq", 1)
                newton_rsqrt(vector, stA, 0, stP[:, 0, 0:1], stP[:, 0, 1:2],
                             out_rs12=stP[:, 0, 2:3], out_nmr12=stP[:, 0, 3:4]
                             ).then_inc(S["a_ln"], 1)

            def kbuild(k):
                cb, dc = k // 8, k % 8
                sem_n, cnt = ksem(k)
                w(vector, sem_n, cnt)
                w(vector, "p_bce", 2 * cb + 2)
                vector.tensor_tensor(KW1[:, dc, cb * 1024:(cb + 1) * 1024], kstg(k),
                                     bc16[:, cb * 1024:(cb + 1) * 1024], OP.mult
                                     ).then_inc(S["p_kc"], 1)

            def sevac0(g):
                w(vector, "a_sd", g + 1)
                vector.tensor_scalar(SS[0][:, g * 512:(g + 1) * 512],
                                     psA[:, g % 2, :], 0.0, NEG, OP.add, OP.max,
                                     accum_out=mpt[:, g:g + 1]).then_inc(S["a_se"], 1)

            for k in range(8):
                kbuild(k)
            if stage >= 2:
                w(vector, "a_xt", 1)
                vector.tensor_copy(xnT[:], psX[:]).then_inc(S["a_xe"], 1)
            for step in range(6):
                for k in range(8 + 4 * step, 12 + 4 * step):
                    kbuild(k)
                if stage >= 2:
                    sevac0(step)
            if stage >= 2:
                sevac0(6)
                sevac0(7)
            for k in range(NT if stage >= 2 else 0):
                par = k % 2
                if k >= 1:
                    w(vector, "a_xt", k + 1)
                    vector.tensor_copy(xnT[:], psX[:]).then_inc(S["a_xe"], 1)
                if k >= 2:
                    w(vector, "a_vd", k - 1)
                    w(vector, "d_s", 16 * (k - 3))
                    vector.tensor_scalar(sgbuf[:, k % 2, 0:512], psV[:, 0:512],
                                         stA[:, k % 2, A_RL:A_RL + 1], None, OP.mult)
                    vector.tensor_scalar(sgbuf[:, k % 2, 512:1024], psV[:, 512:1024],
                                         stA[:, k % 2, A_RL:A_RL + 1], None, OP.mult
                                         ).then_inc(S["a_ve"], 1)
                for kg in (range(8) if k >= 1 else []):
                    if kg == 0:
                        w(vector, "a_pe", k - 1)
                    w(vector, "a_sd", 8 * k + kg + 1)
                    vector.tensor_scalar(SS[par][:, kg * 512:(kg + 1) * 512],
                                         psA[:, kg % 2, :], 0.0, NEG, OP.add, OP.max,
                                         accum_out=mpt[:, kg:kg + 1]
                                         ).then_inc(S["a_se"], 1)
                if k >= 1:
                    w(vector, "a_pe", k)
                    vector.reciprocal(stA[:, 1 - par, A_RL:A_RL + 1],
                                      stA[:, 1 - par, A_L:A_L + 1]).then_inc(S["a_rl"], 1)
                    vector.drain()
                    for tb in range(4):
                        w(vector, "a_pt", 4 * (k - 1) + tb + 1)
                        vector.tensor_copy(PT8[:, tb * 1024:(tb + 1) * 1024],
                                           psT[:, tb % 2, :]).then_inc(S["a_pv"], 1)
                vector.drain()
                vector.reduce_max(stA[:, par, A_NM:A_NM + 1], mpt[:], axis=AX,
                                  negate=True).then_inc(S["a_me"], 1)
                if k + 1 < NT:
                    w(vector, "d_x", 16 * (k + 2))
                    vector.tensor_scalar(scrb[:], xbuf[:, (k + 1) % 2, :], 0.0, 0.0,
                                         OP.add, OP.add,
                                         accum_out=stA[:, (k + 1) % 2, A_SUM:A_SUM + 1])
                    w(vector, "a_sq", k + 2)
                    newton_rsqrt(vector, stA, (k + 1) % 2, stP[:, k + 1, 0:1],
                                 stP[:, k + 1, 1:2], out_rs12=stP[:, k + 1, 2:3],
                                 out_nmr12=stP[:, k + 1, 3:4]).then_inc(S["a_ln"], 1)
            if stage >= 2:
                par = (NT - 1) % 2
                w(vector, "a_pe", NT)
                vector.reciprocal(stA[:, par, A_RL:A_RL + 1], stA[:, par, A_L:A_L + 1]
                                  ).then_inc(S["a_rl"], 1)
                vector.drain()
                for tb in range(4):
                    w(vector, "a_pt", 4 * (NT - 1) + tb + 1)
                    vector.tensor_copy(PT8[:, tb * 1024:(tb + 1) * 1024], psT[:, tb % 2, :]
                                       ).then_inc(S["a_pv"], 1)
                for j in (NT - 2, NT - 1):
                    w(vector, "a_vd", j + 1)
                    w(vector, "d_s", 16 * (j - 1))
                    vector.tensor_scalar(sgbuf[:, j % 2, 0:512], psV[:, 0:512],
                                         stA[:, j % 2, A_RL:A_RL + 1], None, OP.mult)
                    vector.tensor_scalar(sgbuf[:, j % 2, 512:1024], psV[:, 512:1024],
                                         stA[:, j % 2, A_RL:A_RL + 1], None, OP.mult
                                         ).then_inc(S["a_ve"], 1)
            if stage < 3:
                return
            w(vector, "a_wr", 64)
            vector.reduce_sum(accw[:, 0:1], wpart[:, 0:32], axis=AX)
            vector.reduce_sum(accw[:, 1:2], wpart[:, 32:64], axis=AX
                              ).then_inc(S["t_vw"], 1)
            w(vector, "t_pr", 2)
            vector.tensor_copy(wtmp[0:1, 0:2], psR[0:1, 0:2])
            vector.drain()
            vector.tensor_scalar(wtmp[0:1, 2:4], wtmp[0:1, 0:2], 1.0 / (D * H), EPS,
                                 OP.mult, OP.max).then_inc(S["t_vw"], 1)
            w(vector, "t_pb", 2)
            vector.tensor_copy(mwbc, psR[:, 2:4])
            vector.drain()
            vector.reciprocal(swbc, mwbc).then_inc(S["t_sw"], 1)
            vector.drain()
            for c in range(66):
                if c < 64:
                    col = 0 if c < 32 else 1
                    sem_n, cnt = qsem(c)
                    w(vector, sem_n, cnt)
                    src = qslot(c)
                    vector.tensor_scalar(src, src, swbc[:, col:col + 1],
                                         MAGIC, OP.mult, OP.add).then_inc(S["b_q1"], 1)
                if c >= 2:
                    cc = c - 2
                    w(vector, "b_q2", cc + 1)
                    dest = (KW1[:, cc % 8, (cc // 8) * 1024:(cc // 8 + 1) * 1024] if cc < 32
                            else VW2[:, cc - 32, :])
                    vector.tensor_scalar(dest, qslot(cc), -1.0, 1.0, OP.max, OP.min
                                         ).then_inc(S["b_qv"], 1)
            if stage < 4:
                return
            w(vector, "d_s", 16 * NT)
            w(vector, "a_wr", 64)
            w(vector, "a_pv", 4 * NT)
            w(vector, "d_bc", 16 * 4)
            w(vector, "d_wa", 16 * 64)

            def x2evac(im_, hb):
                w(vector, "b_pt", 8 * im_ + hb + 1)
                vector.tensor_copy(x2T[:, hb % 2, :], psT[:, hb % 2, 0:512]
                                   ).then_inc(S["b_pv"], 1)

            def q1_chain(j, ev=None):
                pj = j % 2
                sj = lambda kk: stB[:, pj, kk:kk + 1]

                def step():
                    if ev is not None:
                        hb = next(ev[1], None)
                        if hb is not None:
                            x2evac(ev[0], hb)
                w(vector, "d_x", 16 * (NT + j + 1))
                w(vector, "b_xq", j)
                vector.tensor_scalar(xn_, xbuf[:, j % 3, :], stP[:, j, 2:3],
                                     stP[:, j, 3:4], OP.mult, OP.add)
                step()
                vector.tensor_tensor(xn_, xn_, gbc[:], OP.mult)
                step()
                vector.tensor_tensor(xn_, xn_, bbc[:], OP.add)
                step()
                vector.reduce_max(sj(B_AM1), xn_, axis=AX, apply_absolute_value=True)
                step()
                vector.drain()
                vector.tensor_scalar(sj(B_M1), sj(B_AM1), EPS, None, OP.max)
                vector.drain()
                vector.reciprocal(sj(B_T), sj(B_M1))
                vector.drain()
                vector.tensor_scalar(sj(B_SX1), sj(B_T), 127.0, None, OP.mult)
                vector.tensor_tensor(sj(B_T), sj(B_M1), mwbc[:, 0:1], OP.mult)
                vector.drain()
                vector.tensor_scalar(sj(B_DS1), sj(B_T), 1.0 / 127.0, None, OP.mult)
                vector.drain()
                step()
                vector.tensor_scalar(xn_, xn_, sj(B_SX1), MAGIC, OP.mult, OP.add
                                     ).then_inc(S["b_q1a"], 1)

            q1_chain(0)
            w(vector, "b_xt", 1)
            vector.tensor_copy(xnT[:], psX[:]).then_inc(S["b_xe"], 1)
            for i in range(NT):
                par = i % 2
                sB = lambda kk: stB[:, par, kk:kk + 1]
                sBp = lambda kk: stB[:, 1 - par, kk:kk + 1]
                evs = list(range(8) if i == 0 else range(2, 8))
                ams = list(range(8))
                order = []
                while evs or ams:
                    if evs:
                        order.append(("e", evs.pop(0)))
                    if ams:
                        order.append(("a", ams.pop(0)))
                if i >= 1:
                    ai = [n for n, kg in enumerate(order) if kg == ("a", 6)]
                    order.insert(ai[0], ("h", 0))
                    aj = [n for n, kg in enumerate(order) if kg == ("a", 7)]
                    order.insert(aj[0], ("h", 1))
                for kind, g in order:
                    if kind == "h":
                        x2evac(i - 1, g)
                    elif kind == "e":
                        w(vector, "b_fd", 8 * i + g + 1)
                        if g == 0:
                            w(vector, "b_q2c", i)
                        vector.scalar_tensor_tensor(big0[:, g * 512:(g + 1) * 512],
                                                    psA[:, g % 2, :], sB(B_DS1),
                                                    bc16[:, g * 512:(g + 1) * 512],
                                                    OP.mult, OP.add).then_inc(S["b_fe"], 1)
                    else:
                        w(vector, "b_ge", 8 * i + g + 1)
                        vector.reduce_max(mpt[:, g:g + 1],
                                          big0[:, g * 512:(g + 1) * 512], axis=AX,
                                          apply_absolute_value=True)
                vector.drain()
                vector.reduce_max(sB(B_AM2), mpt[:], axis=AX)
                vector.drain()
                vector.tensor_scalar(sB(B_M2), sB(B_AM2), EPS, None, OP.max)
                vector.drain()
                vector.reciprocal(sB(B_T), sB(B_M2))
                vector.drain()
                vector.tensor_scalar(sB(B_SX2), sB(B_T), 127.0, None, OP.mult)
                vector.tensor_tensor(sB(B_T), sB(B_M2), mwbc[:, 1:2], OP.mult)
                vector.drain()
                vector.tensor_scalar(sB(B_DS2), sB(B_T), 1.0 / 127.0, None, OP.mult
                                     ).then_inc(S["b_q2s"], 1)
                vector.drain()
                it = iter(range(2, 8))
                if i + 1 < NT:
                    q1_chain(i + 1, ev=(i - 1, it) if i >= 1 else None)
                if i >= 1:
                    for hb in it:
                        x2evac(i - 1, hb)
                if i + 1 < NT:
                    w(vector, "b_xt", i + 2)
                    vector.tensor_copy(xnT[:], psX[:]).then_inc(S["b_xe"], 1)
                if i >= 1:
                    w(vector, "b_vd", i)
                    vector.scalar_tensor_tensor(gt_[:, 0:512], psV[:, 0:512], sBp(B_DS2),
                                                b2bc[:, 0:512], OP.mult, OP.add)
                    vector.scalar_tensor_tensor(gt_[:, 512:1024], psV[:, 512:1024],
                                                sBp(B_DS2), b2bc[:, 512:1024],
                                                OP.mult, OP.add).then_inc(S["b_y2"], 1)
                if i + 1 < NT:
                    sN = lambda kk: stB[:, (i + 1) % 2, kk:kk + 1]
                    for hg in range(2):
                        if hg == 0:
                            w(vector, "b_q2c", i + 1)
                        w(vector, "b_fd", 8 * (i + 1) + hg + 1)
                        vector.scalar_tensor_tensor(big0[:, hg * 512:(hg + 1) * 512],
                                                    psA[:, hg % 2, :], sN(B_DS1),
                                                    bc16[:, hg * 512:(hg + 1) * 512],
                                                    OP.mult, OP.add).then_inc(S["b_fe"], 1)
                if i >= 1:
                    w(vector, "b_sl", i)
                    w(vector, "d_sr", 16 * i)
                    vector.tensor_tensor(gt_, gt_, sgbf[:, (i - 1) % 2, :], OP.mult)
                    vector.tensor_tensor(xbuf[:, (i - 1) % 3, :], xbuf[:, (i - 1) % 3, :],
                                         gt_, OP.add).then_inc(S["b_fin"], 1)
            par = (NT - 1) % 2
            sB = lambda kk: stB[:, par, kk:kk + 1]
            for hb in range(8):
                w(vector, "b_pt", 8 * (NT - 1) + hb + 1)
                vector.tensor_copy(x2T[:, hb % 2, :], psT[:, hb % 2, 0:512]
                                   ).then_inc(S["b_pv"], 1)
            w(vector, "b_vd", NT)
            vector.scalar_tensor_tensor(gt_[:, 0:512], psV[:, 0:512], sB(B_DS2),
                                        b2bc[:, 0:512], OP.mult, OP.add)
            vector.scalar_tensor_tensor(gt_[:, 512:1024], psV[:, 512:1024], sB(B_DS2),
                                        b2bc[:, 512:1024], OP.mult, OP.add
                                        ).then_inc(S["b_y2"], 1)
            w(vector, "b_sl", NT)
            w(vector, "d_sr", 16 * NT)
            vector.tensor_tensor(gt_, gt_, sgbf[:, (NT - 1) % 2, :], OP.mult)
            vector.tensor_tensor(xbuf[:, (NT - 1) % 3, :], xbuf[:, (NT - 1) % 3, :],
                                 gt_, OP.add).then_inc(S["b_fin"], 1)

        @blk.scalar
        def _(act):
            act.dma_start(out=kstg(1), in_=KTd[128:256, 0:1024]
                          ).then_inc(S[ksem(1)[0]], 16)
            act.dma_start(out=kstg(3), in_=KTd[384:512, 0:1024]
                          ).then_inc(S[ksem(3)[0]], 16)
            w(act, "p_s4a", 1)
            act.activation(scrb[0:4, :], xbuf[0:4, 0, :], AF.Square,
                           accum_out=stA[0:4, 0, A_SSQ:A_SSQ + 1]).then_inc(S["p_sq4"], 1)
            w(act, "p_ln4", 1)
            act.activation(xbuf[0:4, 0, :], xbuf[0:4, 0, :], AF.Identity,
                           bias=stA[0:4, 0, A_NM:A_NM + 1],
                           scale=stA[0:4, 0, A_Y:A_Y + 1]).then_inc(S["p_x4"], 1)
            if stage >= 2:
                w(act, "d_x", 16)
                act.activation(scrb[:], xbuf[:, 0, :], AF.Square,
                               accum_out=stA[:, 0, A_SSQ:A_SSQ + 1]).then_inc(S["a_sq"], 1)
                w(act, "a_ln", 1)
                act.activation(xnb[:], xbuf[:, 0, :], AF.Identity,
                               bias=stP[:, 0, 1:2], scale=stP[:, 0, 0:1]
                               ).then_inc(S["a_xa"], 1)
            for k in range(5, 32, 2):
                w(act, "p_kc", k - 3)
                act.dma_start(out=kstg(k),
                              in_=KTd[(k % 8) * 128:(k % 8 + 1) * 128,
                                      (k // 8) * 1024:(k // 8 + 1) * 1024]
                              ).then_inc(S[ksem(k)[0]], 16)
            for k in range(NT if stage >= 2 else 0):
                par = k % 2
                if k >= 1:
                    if k == 1:
                        w(act, "p_kc", 30)
                    w(act, "a_me", k)
                    w(act, "a_pt", 4 * (k - 1))
                    act.activation(P8[:], SS[1 - par][:], AF.Exp,
                                   bias=stA[:, 1 - par, A_NM:A_NM + 1],
                                   accum_out=stA[:, 1 - par, A_L:A_L + 1]
                                   ).then_inc(S["a_pe"], 1)
                if k + 1 < NT:
                    w(act, "d_x", 16 * (k + 2))
                    act.activation(scrb[:], xbuf[:, (k + 1) % 2, :], AF.Square,
                                   accum_out=stA[:, (k + 1) % 2, A_SSQ:A_SSQ + 1]
                                   ).then_inc(S["a_sq"], 1)
                    w(act, "a_ln", k + 2)
                    w(act, "a_xt", k + 1)
                    act.activation(xnb[:], xbuf[:, (k + 1) % 2, :], AF.Identity,
                                   bias=stP[:, k + 1, 1:2], scale=stP[:, k + 1, 0:1]
                                   ).then_inc(S["a_xa"], 1)
                for kk in (2 * k, 2 * k + 1):
                    w(act, "d_wa", 16 * (kk + 1))
                    act.activation(scrb[:], absg[:, kk % 2, :], AF.Abs,
                                   accum_out=wpart[:, kk:kk + 1]).then_inc(S["a_wr"], 1)
                if k >= 2:
                    w(act, "a_ve", k - 1)
                    act.dma_start(out=OUTd[128 * (k - 2):128 * (k - 1), :],
                                  in_=sgbuf[:, k % 2, :]).then_inc(S["d_s"], 16)
            if stage >= 2:
                par = (NT - 1) % 2
                w(act, "a_me", NT)
                w(act, "a_pt", 4 * (NT - 1))
                act.activation(P8[:], SS[par][:], AF.Exp, bias=stA[:, par, A_NM:A_NM + 1],
                               accum_out=stA[:, par, A_L:A_L + 1]).then_inc(S["a_pe"], 1)
                for j in (NT - 2, NT - 1):
                    w(act, "a_ve", j + 1)
                    act.dma_start(out=OUTd[128 * j:128 * (j + 1), :],
                                  in_=sgbuf[:, j % 2, :]).then_inc(S["d_s"], 16)
            if stage >= 3:
                act.dma_start(out=qslot(1), in_=wchunk_src(1)
                              ).then_inc(S[qsem(1)[0]], 16)
                act.dma_start(out=qslot(3), in_=wchunk_src(3)
                              ).then_inc(S[qsem(3)[0]], 16)
            for j in range(64 if stage >= 3 else 0):
                w(act, "b_q1", j + 1)
                act.activation(qslot(j), qslot(j), AF.Copy, bias=-MAGIC
                               ).then_inc(S["b_q2"], 1)
                c = j + 6
                if c < 64 and qname(c) == "d_pb3":
                    if c == 11:
                        w(act, "a_pv", 4 * NT)
                    w(act, "b_qv", c - 5)
                    act.dma_start(out=qslot(c), in_=wchunk_src(c)
                                  ).then_inc(S[qsem(c)[0]], 16)
            if stage < 4:
                return
            w(act, "b_q1a", 1)
            act.activation(xnb[:], xn_, AF.Copy, bias=-MAGIC).then_inc(S["b_xq"], 1)
            for i in range(NT):
                par = i % 2
                for hg in range(8):
                    w(act, "b_fe", 8 * i + hg + 1)
                    act.activation(big0[:, hg * 512:(hg + 1) * 512],
                                   big0[:, hg * 512:(hg + 1) * 512], AF.Gelu
                                   ).then_inc(S["b_ge"], 1)
                w(act, "b_q2s", i + 1)
                act.activation(big0[:], big0[:], AF.Identity,
                               bias=magicc, scale=stB[:, par, B_SX2:B_SX2 + 1])
                if i + 1 < NT:
                    w(act, "b_q1a", i + 2)
                    w(act, "b_xt", i + 1)
                    act.activation(xnb[:], xn_, AF.Copy, bias=-MAGIC).then_inc(S["b_xq"], 1)
                w(act, "b_pt", 8 * i)
                act.activation(x2q[:, :, :], big0[:], AF.Copy, bias=-MAGIC
                               ).then_inc(S["b_q2c"], 1)
                if i >= 1:
                    w(act, "b_y2", i)
                    act.activation(gt_, gt_, AF.Silu).then_inc(S["b_sl"], 1)
            w(act, "b_y2", NT)
            act.activation(gt_, gt_, AF.Silu).then_inc(S["b_sl"], 1)

        @blk.tensor
        def _(pe):
            w(pe, "g_init", 4)
            w(pe, "d_cat", 64)
            for g in range(8):
                w(pe, "p_bce", g - 1)
                pe.matmul(psA[:, g % 2, :], onesr[0:1, 0:128],
                          big0[0:1, g * 512:(g + 1) * 512], start=True, stop=True
                          ).then_inc(S["p_bc"], 1)
            for k in range(NT if stage >= 2 else 0):
                par = k % 2
                w(pe, "a_xa", k + 1)
                w(pe, "a_xe", k)
                if k == 0:
                    w(pe, "v_init", 1)
                for dc in range(8):
                    last = pe.transpose(psX[:, dc * 128:(dc + 1) * 128],
                                        xnb[:, dc * 128:(dc + 1) * 128], ident[:])
                last.then_inc(S["a_xt"], 1)
                for kg in range(8):
                    if k == 0:
                        if kg == 0:
                            w(pe, "p_bce", 8)
                        w(pe, "p_kc", 8 * (kg // 2 + 1))
                    if kg == 0:
                        if k == 1:
                            w(pe, "p_kc", 32)
                        w(pe, "a_xe", k + 1)
                    w(pe, "a_se", 8 * k + kg - 1)
                    for dc in range(8):
                        last = pe.matmul(psA[:, kg % 2, :], xnT[:, dc * 128:(dc + 1) * 128],
                                         KW1[:, dc, kg * 512:(kg + 1) * 512],
                                         start=(dc == 0), stop=(dc == 7))
                    last.then_inc(S["a_sd"], 1)
                if k >= 1:
                    km = k - 1
                    for tb in range(4):
                        if tb == 0:
                            w(pe, "a_pe", k)
                        w(pe, "a_pv", 4 * km + tb - 1)
                        for tc in range(8):
                            kc = 8 * tb + tc
                            last = pe.transpose(psT[:, tb % 2, tc * 128:(tc + 1) * 128],
                                                P8[:, kc * 128:(kc + 1) * 128], ident[:])
                        last.then_inc(S["a_pt"], 1)
                    w(pe, "a_ve", km)
                    for kc in range(32):
                        if km == 0:
                            w(pe, ("d_v", "d_va", "d_vb")[kc % 3], 16 * (kc // 3 + 1))
                        if kc % 8 == 0:
                            w(pe, "a_pv", 4 * km + kc // 8 + 1)
                        pe.matmul(psV[:, 0:512], PT8[:, kc * 128:(kc + 1) * 128],
                                  VW2[:, kc, 0:512], start=(kc == 0), stop=(kc == 31))
                        last = pe.matmul(psV[:, 512:1024], PT8[:, kc * 128:(kc + 1) * 128],
                                         VW2[:, kc, 512:1024], start=(kc == 0),
                                         stop=(kc == 31))
                    last.then_inc(S["a_vd"], 1)
            if stage >= 2:
                km = NT - 1
                for tb in range(4):
                    if tb == 0:
                        w(pe, "a_pe", NT)
                    w(pe, "a_pv", 4 * km + tb - 1)
                    for tc in range(8):
                        kc = 8 * tb + tc
                        last = pe.transpose(psT[:, tb % 2, tc * 128:(tc + 1) * 128],
                                            P8[:, kc * 128:(kc + 1) * 128], ident[:])
                    last.then_inc(S["a_pt"], 1)
                w(pe, "a_ve", km)
                for kc in range(32):
                    if kc % 8 == 0:
                        w(pe, "a_pv", 4 * km + kc // 8 + 1)
                    pe.matmul(psV[:, 0:512], PT8[:, kc * 128:(kc + 1) * 128],
                              VW2[:, kc, 0:512], start=(kc == 0), stop=(kc == 31))
                    last = pe.matmul(psV[:, 512:1024], PT8[:, kc * 128:(kc + 1) * 128],
                                     VW2[:, kc, 512:1024], start=(kc == 0), stop=(kc == 31))
                last.then_inc(S["a_vd"], 1)
            if stage < 3:
                return
            w(pe, "t_vw", 1)
            pe.matmul(psR[0:1, 0:1], accw[:, 0:1], ones, start=True, stop=True
                      ).then_inc(S["t_pr"], 1)
            pe.matmul(psR[0:1, 1:2], accw[:, 1:2], ones, start=True, stop=True
                      ).then_inc(S["t_pr"], 1)
            w(pe, "t_vw", 2)
            pe.matmul(psR[:, 2:3], onesr[0:1, :], wtmp[0:1, 2:3], start=True, stop=True
                      ).then_inc(S["t_pb"], 1)
            pe.matmul(psR[:, 3:4], onesr[0:1, :], wtmp[0:1, 3:4], start=True, stop=True
                      ).then_inc(S["t_pb"], 1)
            if stage < 4:
                return
            w(pe, "b_xq", 1)
            for dc in range(8):
                last = pe.transpose(psX[:, dc * 128:(dc + 1) * 128],
                                    xnb[:, dc * 128:(dc + 1) * 128], ident[:])
            last.then_inc(S["b_xt"], 1)
            for i in range(NT):
                par = i % 2
                for hg in range(8):
                    if i == 0:
                        w(pe, "b_qv", 8 * (hg // 2) + 8)
                    if hg == 0:
                        w(pe, "b_xe", i + 1)
                    w(pe, "b_fe", 8 * i + hg - 1)
                    for dc in range(8):
                        last = pe.matmul(psA[:, hg % 2, :], xnT[:, dc * 128:(dc + 1) * 128],
                                         KW1[:, dc, hg * 512:(hg + 1) * 512],
                                         start=(dc == 0), stop=(dc == 7))
                    last.then_inc(S["b_fd"], 1)
                if i == 0:
                    w(pe, "b_xq", 2)
                    for dc in range(8):
                        last = pe.transpose(psX[:, dc * 128:(dc + 1) * 128],
                                            xnb[:, dc * 128:(dc + 1) * 128], ident[:])
                    last.then_inc(S["b_xt"], 1)
                if i >= 1:
                    im = i - 1
                    if im == 0:
                        w(pe, "b_qv", 64)
                    w(pe, "b_y2", im)
                    for hb in range(9):
                        if hb < 8:
                            if hb == 0:
                                w(pe, "b_q2c", im + 1)
                            w(pe, "b_pv", 8 * im + hb - 1)
                            for c in range(4):
                                last = pe.transpose(psT[:, hb % 2, c * 128:(c + 1) * 128],
                                                    x2q[:, hb, c * 128:(c + 1) * 128],
                                                    ident[:])
                            last.then_inc(S["b_pt"], 1)
                        if hb == 8 and i + 1 < NT:
                            w(pe, "b_xq", i + 2)
                            for dc in range(8):
                                last = pe.transpose(psX[:, dc * 128:(dc + 1) * 128],
                                                    xnb[:, dc * 128:(dc + 1) * 128],
                                                    ident[:])
                            last.then_inc(S["b_xt"], 1)
                        if hb >= 1:
                            hq = hb - 1
                            w(pe, "b_pv", 8 * im + hq + 1)
                            for c in range(4):
                                kc = 4 * hq + c
                                pe.matmul(psV[:, 0:512], x2T[:, hq % 2, c * 128:(c + 1) * 128],
                                          VW2[:, kc, 0:512], start=(kc == 0), stop=(kc == 31))
                                last = pe.matmul(psV[:, 512:1024],
                                                 x2T[:, hq % 2, c * 128:(c + 1) * 128],
                                                 VW2[:, kc, 512:1024],
                                                 start=(kc == 0), stop=(kc == 31))
                    last.then_inc(S["b_vd"], 1)
            im = NT - 1
            w(pe, "b_y2", im)
            for hb in range(9):
                if hb < 8:
                    if hb == 0:
                        w(pe, "b_q2c", im + 1)
                    w(pe, "b_pv", 8 * im + hb - 1)
                    for c in range(4):
                        last = pe.transpose(psT[:, hb % 2, c * 128:(c + 1) * 128],
                                            x2q[:, hb, c * 128:(c + 1) * 128], ident[:])
                    last.then_inc(S["b_pt"], 1)
                if hb >= 1:
                    hq = hb - 1
                    w(pe, "b_pv", 8 * im + hq + 1)
                    for c in range(4):
                        kc = 4 * hq + c
                        pe.matmul(psV[:, 0:512], x2T[:, hq % 2, c * 128:(c + 1) * 128],
                                  VW2[:, kc, 0:512], start=(kc == 0), stop=(kc == 31))
                        last = pe.matmul(psV[:, 512:1024],
                                         x2T[:, hq % 2, c * 128:(c + 1) * 128],
                                         VW2[:, kc, 512:1024], start=(kc == 0),
                                         stop=(kc == 31))
            last.then_inc(S["b_vd"], 1)

    return nc


_NC_CACHE = []


def kernel(L0_attn_q, L0_attn_k, L0_attn_v, L0_attn_scale_factor, L0_attn_dropout_p,
           L0_ffn_w1, L0_ffn_b1, L0_ffn_w2, L0_ffn_b2, L0_ffn_gamma, L0_ffn_beta):
    q = np.ascontiguousarray(np.asarray(L0_attn_q))
    kT = np.ascontiguousarray(np.asarray(L0_attn_k)[0].T)
    v0 = np.ascontiguousarray(np.asarray(L0_attn_v)[0])
    w1T = np.ascontiguousarray(np.asarray(L0_ffn_w1).T)
    w2T = np.ascontiguousarray(np.asarray(L0_ffn_w2).T)
    b1bc = np.ascontiguousarray(np.broadcast_to(np.asarray(L0_ffn_b1)[None, :], (128, H)))
    b2bc = np.ascontiguousarray(np.broadcast_to(np.asarray(L0_ffn_b2)[None, :], (128, D)))
    gbc = np.ascontiguousarray(np.broadcast_to(np.asarray(L0_ffn_gamma)[None, :], (128, D)))
    bbc = np.ascontiguousarray(np.broadcast_to(np.asarray(L0_ffn_beta)[None, :], (128, D)))

    if not _NC_CACHE:
        _NC_CACHE.append(build_program())
    nc = _NC_CACHE[0]
    in_maps = [dict(x=np.ascontiguousarray(q[c]), kT=kT, v=v0, w1T=w1T, w2T=w2T,
                    b1bc=b1bc, b2bc=b2bc, gbc=gbc, bbc=bbc) for c in range(8)]
    res = run_bass_kernel_spmd(nc, in_maps, core_ids=list(range(8)))
    return np.stack([np.asarray(res.results[c]["out"]) for c in range(8)], axis=0)


# version 24
# speedup vs baseline: 1.0196x; 1.0132x over previous
import sys
if "/opt/trn_rl_repo" not in sys.path:
    sys.path.insert(0, "/opt/trn_rl_repo")
import numpy as np
import concourse.bass as bass
import concourse.mybir as mybir
from concourse.bass_utils import run_bass_kernel_spmd
from concourse.alu_op_type import AluOpType as OP

F32 = mybir.dt.float32
BF16 = mybir.dt.bfloat16
AF = mybir.ActivationFunctionType
AX = mybir.AxisListType.X

MAGIC = 1.5 * 2.0 ** 23
EPS = 1e-5
NT = 32
D = 1024
SK = 4096
H = 4096
NEG = -3.0e38


def build_program(stage=4):
    nc = bass.Bass("TRN2", target_bir_lowering=False, detect_race_conditions=False)

    Xd = nc.dram_tensor("x", [SK, D], F32, kind="ExternalInput").ap()
    KTd = nc.dram_tensor("kT", [D, SK], F32, kind="ExternalInput").ap()
    Vd = nc.dram_tensor("v", [SK, D], F32, kind="ExternalInput").ap()
    W1Td = nc.dram_tensor("w1T", [D, H], F32, kind="ExternalInput").ap()
    W2Td = nc.dram_tensor("w2T", [H, D], F32, kind="ExternalInput").ap()
    B1d = nc.dram_tensor("b1bc", [128, H], F32, kind="ExternalInput").ap()
    B2d = nc.dram_tensor("b2bc", [128, D], F32, kind="ExternalInput").ap()
    Gd = nc.dram_tensor("gbc", [128, D], F32, kind="ExternalInput").ap()
    Bd = nc.dram_tensor("bbc", [128, D], F32, kind="ExternalInput").ap()
    OUTd = nc.dram_tensor("out", [SK, D], F32, kind="ExternalOutput").ap()

    cur = [(nc.sbuf_base + 63) // 64 * 64]
    offs = {}

    def sb(name, shape, dt, alias_at=None):
        nbytes = int(np.prod(shape[1:])) * (2 if dt == BF16 else 4)
        off = cur[0] if alias_at is None else alias_at
        t = nc.alloc_sbuf_tensor_at(name, shape, dt, offset=off).ap()
        if alias_at is None:
            cur[0] += (nbytes + 31) // 32 * 32
        offs[name] = off
        return t

    KW1 = sb("KW1", [128, 8, SK], BF16)
    VW2 = sb("VW2", [128, 32, D], BF16)
    big0 = sb("big0", [128, SK], F32)
    bc16 = sb("bc16", [128, SK], F32)
    P8 = sb("P8", [128, SK], BF16)
    gbc = sb("gbc_s", [128, D], F32, alias_at=offs["P8"])
    bbc = sb("bbc_s", [128, D], F32, alias_at=offs["P8"] + 4096)
    stg45 = sb("stg45", [128, 2, 1024], F32, alias_at=offs["P8"])

    def kstg(k):
        s = k % 6
        if s < 2:
            return stg45[:, s, :]
        if s < 4:
            return stgPT[:, s - 2, :]
        return big0[:, (s - 2) * 1024:(s - 1) * 1024]

    def ksem(k):
        p = k // 2
        sems = ("d_ka", "d_kb", "d_kc") if k % 2 == 0 else ("d_kd", "d_ke", "d_kf")
        return sems[p % 3], 16 * (p // 3 + 1)

    def qslot(c):
        s = c % 6
        return big0[:, s * 1024:(s + 1) * 1024] if s < 4 else stg45[:, s - 4, :]

    QPAT = ("d_pb", "d_pb3", "d_pb", "d_pb3", "d_pb2")

    def qname(c):
        return QPAT[c % 5]

    def qpos(c):
        return sum(1 for j in range(c + 1) if QPAT[j % 5] == QPAT[c % 5])

    QSEMS = {"d_pb": ("d_pb", "d_pba", "d_pbb"),
             "d_pb3": ("d_pb3", "d_p3a", "d_p3b"),
             "d_pb2": ("d_pb2",)}

    def qsem(c):
        q = qname(c)
        p = qpos(c) - 1
        sems = QSEMS[q]
        return sems[p % len(sems)], 16 * (p // len(sems) + 1)
    PT8 = sb("PT8", [128, SK], BF16)
    x2q = sb("x2q", [128, 8, 512], BF16, alias_at=offs["PT8"])
    stgPT = sb("stgPT", [128, 2, 1024], F32, alias_at=offs["PT8"])
    xbuf = sb("xbuf", [128, 3, D], F32)
    sgbuf = sb("sgbuf", [128, 2, D], F32)
    sgbf = sb("sgbf", [128, 2, D], BF16, alias_at=offs["sgbuf"])
    x2T = sb("x2T", [128, 2, 512], BF16, alias_at=offs["sgbuf"] + 4096)
    xng = sb("xng", [128, D], F32)
    gtb = sb("gtb", [128, D], BF16, alias_at=offs["sgbuf"] + 6144)
    absg = sb("absg", [128, 2, D], BF16, alias_at=offs["xng"])
    xnb = sb("xnb", [128, D], BF16)
    xnT = sb("xnT", [128, D], BF16)
    scrb = sb("scrb", [128, D], BF16)
    b2bc = sb("b2bc_s", [128, D], BF16, alias_at=offs["scrb"])
    stP = sb("stP", [128, NT, 4], F32)
    stA = sb("stA", [128, 2, 9], F32)
    stB = sb("stB", [128, 2, 9], F32)
    mpt = sb("mpt", [128, 8], F32)
    accw = sb("accw", [128, 2], F32)
    wpart = sb("wpart", [128, 64], F32)
    wtmp = sb("wtmp", [128, 4], F32)
    msw = sb("msw", [128, 4], F32)
    mwbc = msw[:, 0:2]
    swbc = msw[:, 2:4]
    onesm = sb("onesm", [128, 2], F32)
    ones = onesm[:, 0:1]
    magicc = onesm[:, 1:2]
    onesr = sb("onesr", [1, 128], F32)
    ident = sb("ident", [128, 128], BF16)
    assert cur[0] <= 229376, f"SBUF over budget: {cur[0]}"

    xn_ = xng[:, 0:D]
    gt_ = gtb[:, 0:D]
    SS = [big0, bc16]

    psA = nc.alloc_psum_tensor("psA", [128, 2, 512], F32).ap()
    psV = nc.alloc_psum_tensor("psV", [128, 1024], F32).ap()
    psT = nc.alloc_psum_tensor("psT", [128, 2, 1024], BF16).ap()
    psX = nc.alloc_psum_tensor("psX", [128, 1024], BF16).ap()
    psR = nc.alloc_psum_tensor("psR", [128, 16], F32).ap()

    semnames = ("d_x d_pa d_v d_wa d_s d_pb d_bc d_sr d_o d_cat "
                "a_ln a_xa a_xt a_xe a_sd a_se a_me a_pe a_rl a_pt a_pv a_vd a_ve a_wr "
                "g_init v_init p_ln4 p_x4 p_bc p_bce p_kc "
                "t_pr t_vw t_pb t_sw b_q1 b_q2 b_qv "
                "b_l1a b_l2 b_h2 b_q1a b_xq b_xt b_xe b_fd b_fe b_ge b_q2s "
                "b_q2c b_pt b_pv b_vd b_y2 b_sl b_fin a_sq b_sq p_sq4 p_s4a d_pb2 "
                "d_pb3 d_pba d_pbb d_p3a d_p3b d_ka d_kb d_kc d_kd d_ke d_kf d_va d_vb").split()
    S = {n: nc.alloc_semaphore(n) for n in semnames}

    def w(eng, name, v):
        if v > 0:
            eng.wait_ge(S[name], v)

    A_SUM, A_SSQ, A_MU, A_W, A_Y, A_T, A_NM, A_L, A_RL = 0, 1, 2, 3, 4, 5, 6, 7, 8
    B_DS2, B_T, B_AM1, B_M1, B_SX1, B_DS1, B_AM2, B_M2, B_SX2 = range(9)

    def newton_rsqrt(vector, st, par, out_rs, out_negmurs, npart=128, iters=3,
                 out_rs12=None, out_nmr12=None):
        sl = lambda k: st[0:npart, par, k:k + 1]
        dr = vector.drain
        dr()
        vector.tensor_scalar(sl(A_MU), sl(A_SUM), 1.0 / D, None, OP.mult)
        vector.tensor_scalar(sl(A_W), sl(A_SSQ), 1.0 / D, None, OP.mult)
        dr()
        vector.tensor_tensor(sl(A_T), sl(A_MU), sl(A_MU), OP.mult)
        dr()
        vector.tensor_tensor(sl(A_W), sl(A_W), sl(A_T), OP.subtract)
        dr()
        vector.tensor_scalar(sl(A_W), sl(A_W), EPS, None, OP.add)
        dr()
        vector.tensor_scalar(sl(A_Y), sl(A_W), -0.5, 1.5, OP.mult, OP.add)
        dr()
        for _ in range(iters - 1):
            vector.tensor_tensor(sl(A_T), sl(A_Y), sl(A_Y), OP.mult)
            dr()
            vector.tensor_tensor(sl(A_T), sl(A_T), sl(A_W), OP.mult)
            dr()
            vector.tensor_scalar(sl(A_T), sl(A_T), -0.5, 1.5, OP.mult, OP.add)
            dr()
            vector.tensor_tensor(sl(A_Y), sl(A_Y), sl(A_T), OP.mult)
            dr()
        vector.tensor_copy(out_rs, sl(A_Y))
        vector.tensor_tensor(sl(A_T), sl(A_MU), sl(A_Y), OP.mult)
        dr()
        last = vector.tensor_scalar(out_negmurs, sl(A_T), -1.0, None, OP.mult)
        if out_rs12 is None:
            return last
        vector.reciprocal(sl(A_MU), sl(A_W))
        dr()
        vector.tensor_scalar(sl(A_MU), sl(A_MU), -EPS, 1.0, OP.mult, OP.add)
        dr()
        vector.tensor_scalar(sl(A_MU), sl(A_MU), -0.5, 1.5 - 0.5 * EPS, OP.mult, OP.add)
        dr()
        vector.tensor_tensor(out_rs12, out_rs, sl(A_MU), OP.mult)
        return vector.tensor_tensor(out_nmr12, out_negmurs, sl(A_MU), OP.mult)

    def wchunk_src(k):
        if k < 32:
            dc, colblk = k % 8, k // 8
            return W1Td[128 * dc:128 * dc + 128,
                        1024 * colblk:1024 * colblk + 1024]
        j = k - 32
        return W2Td[128 * j:128 * j + 128, :]

    X4view = Xd.rearrange("(a b) d -> a b d", b=1024)[:, 0, :]

    with nc.Block() as blk:
        @blk.sync
        def _(sync):
            sync.dma_start(out=xbuf[0:4, 0, :], in_=X4view).then_inc(S["d_pa"], 16)
            if stage >= 2:
                w(sync, "d_cat", 64)
                sync.dma_start(out=xbuf[:, 0, :], in_=Xd[0:128, :]).then_inc(S["d_x"], 16)
                w(sync, "d_x", 16)
                sync.dma_start(out=xbuf[:, 1, :], in_=Xd[128:256, :]).then_inc(S["d_x"], 16)
            for k in range(0, 32, 2):
                if k == 4:
                    w(sync, "p_bce", 8)
                w(sync, "p_kc", k - 5)
                sync.dma_start(out=kstg(k),
                               in_=KTd[(k % 8) * 128:(k % 8 + 1) * 128,
                                       (k // 8) * 1024:(k // 8 + 1) * 1024]
                               ).then_inc(S[ksem(k)[0]], 16)
            for i in (range(2, NT) if stage >= 2 else []):
                w(sync, "a_xa", i - 1)
                w(sync, "d_x", 16 * i)
                sync.dma_start(out=xbuf[:, i % 2, :], in_=Xd[128 * i:128 * (i + 1), :]
                               ).then_inc(S["d_x"], 16)
            for c in range(64 if stage >= 3 else 0):
                if qname(c) != "d_pb":
                    continue
                if c == 0:
                    w(sync, "a_pe", NT - 1)
                if c == 5:
                    w(sync, "a_pv", 4 * NT)
                w(sync, "b_qv", c - 5)
                sync.dma_start(out=qslot(c), in_=wchunk_src(c)
                               ).then_inc(S[qsem(c)[0]], 16)
            for i in range(NT if stage >= 4 else 0):
                w(sync, "d_o", 16 * (i - 2))
                w(sync, "d_x", 16 * (NT + i))
                sync.dma_start(out=xbuf[:, i % 3, :], in_=Xd[128 * i:128 * (i + 1), :]
                               ).then_inc(S["d_x"], 16)

        @blk.gpsimd
        def _(gps):
            gps.iota(ident[:], [[1, 128]], channel_multiplier=-1,
                     allow_small_or_imprecise_dtypes=True).then_inc(S["g_init"], 1)
            gps.memset(onesm[:, 0:1], 1.0).then_inc(S["g_init"], 1)
            gps.memset(onesr[:], 1.0).then_inc(S["g_init"], 1)
            gps.memset(onesm[:, 1:2], MAGIC).then_inc(S["g_init"], 1)
            w(gps, "p_x4", 1)
            for j in range(4):
                gps.dma_start(out=big0[0:1, j * 1024:(j + 1) * 1024],
                              in_=xbuf[j:j + 1, 0, 0:1024]).then_inc(S["d_cat"], 16)
            for kc in range(32):
                w(gps, ("d_v", "d_va", "d_vb")[kc % 3], 16 * (kc // 3))
                gps.dma_start(out=VW2[:, kc, :], in_=Vd[128 * kc:128 * kc + 128, :]
                              ).then_inc(S[("d_v", "d_va", "d_vb")[kc % 3]], 16)
            for i in range(NT if stage >= 2 else 0):
                for k in (2 * i, 2 * i + 1):
                    w(gps, "a_wr", k)
                    gps.dma_start(out=absg[:, k % 2, :], in_=wchunk_src(k)
                                  ).then_inc(S["d_wa"], 16)
            if stage >= 3:
                for c in range(4, 64, 5):
                    if c == 4:
                        w(gps, "a_pv", 4 * NT)
                    w(gps, "b_qv", c - 4)
                    gps.dma_start(out=qslot(c), in_=wchunk_src(c)
                                  ).then_inc(S[qsem(c)[0]], 16)
            if stage < 4:
                if stage >= 2:
                    gps.wait_ge(S["d_s"], 16 * NT)
                return
            w(gps, "a_pe", NT)
            w(gps, "a_vd", NT)
            w(gps, "a_pv", 4 * NT)
            w(gps, "a_wr", 64)
            w(gps, "d_s", 16 * NT)
            gps.dma_start(out=bc16[:], in_=B1d).then_inc(S["d_bc"], 16)
            w(gps, "b_qv", 60)
            gps.dma_start(out=gbc[:], in_=Gd).then_inc(S["d_bc"], 16)
            gps.dma_start(out=bbc[:], in_=Bd).then_inc(S["d_bc"], 16)
            gps.dma_start(out=b2bc[:], in_=B2d).then_inc(S["d_bc"], 16)
            for i in range(NT):
                w(gps, "b_fin", i - 1)
                w(gps, "d_sr", 16 * i)
                gps.dma_start(out=sgbf[:, i % 2, :], in_=OUTd[128 * i:128 * (i + 1), :]
                              ).then_inc(S["d_sr"], 16)
                if i >= 1:
                    w(gps, "b_fin", i)
                    w(gps, "d_o", 16 * (i - 1))
                    gps.dma_start(out=OUTd[128 * (i - 1):128 * i, :],
                                  in_=xbuf[:, (i - 1) % 3, :]).then_inc(S["d_o"], 16)
            w(gps, "b_fin", NT)
            w(gps, "d_o", 16 * (NT - 1))
            gps.dma_start(out=OUTd[128 * (NT - 1):128 * NT, :],
                          in_=xbuf[:, (NT - 1) % 3, :]).then_inc(S["d_o"], 16)
            gps.wait_ge(S["d_o"], 16 * NT)

        @blk.vector
        def _(vector):
            w(vector, "g_init", 4)
            vector.tensor_scalar(ident[:], ident[:], 0.0, None, OP.is_equal
                                 ).then_inc(S["v_init"], 1)
            w(vector, "d_pa", 16)
            vector.tensor_scalar(scrb[0:4, :], xbuf[0:4, 0, :], 0.0, 0.0, OP.add, OP.add,
                                 accum_out=stA[0:4, 0, A_SUM:A_SUM + 1]).then_inc(S["p_s4a"], 1)
            w(vector, "p_sq4", 1)
            newton_rsqrt(vector, stA, 0, stA[0:4, 0, A_Y:A_Y + 1],
                         stA[0:4, 0, A_NM:A_NM + 1], npart=4).then_inc(S["p_ln4"], 1)
            for g in range(8):
                w(vector, "p_bc", g + 1)
                vector.tensor_copy(bc16[:, g * 512:(g + 1) * 512], psA[:, g % 2, :]
                                   ).then_inc(S["p_bce"], 1)
            if stage >= 2:
                w(vector, "d_x", 16)
                vector.tensor_scalar(scrb[:], xbuf[:, 0, :], 0.0, 0.0, OP.add, OP.add,
                                     accum_out=stA[:, 0, A_SUM:A_SUM + 1])
                w(vector, "a_sq", 1)
                newton_rsqrt(vector, stA, 0, stP[:, 0, 0:1], stP[:, 0, 1:2],
                             out_rs12=stP[:, 0, 2:3], out_nmr12=stP[:, 0, 3:4]
                             ).then_inc(S["a_ln"], 1)

            def kbuild(k):
                cb, dc = k // 8, k % 8
                sem_n, cnt = ksem(k)
                w(vector, sem_n, cnt)
                w(vector, "p_bce", 2 * cb + 2)
                vector.tensor_tensor(KW1[:, dc, cb * 1024:(cb + 1) * 1024], kstg(k),
                                     bc16[:, cb * 1024:(cb + 1) * 1024], OP.mult
                                     ).then_inc(S["p_kc"], 1)

            def sevac0(g):
                w(vector, "a_sd", g + 1)
                vector.tensor_scalar(SS[0][:, g * 512:(g + 1) * 512],
                                     psA[:, g % 2, :], 0.0, NEG, OP.add, OP.max,
                                     accum_out=mpt[:, g:g + 1]).then_inc(S["a_se"], 1)

            for k in range(8):
                kbuild(k)
            if stage >= 2:
                w(vector, "a_xt", 1)
                vector.tensor_copy(xnT[:], psX[:]).then_inc(S["a_xe"], 1)
            for step in range(4):
                for k in range(8 + 4 * step, 12 + 4 * step):
                    kbuild(k)
                if stage >= 2:
                    sevac0(step)
            for k in range(24, 32):
                kbuild(k)
            if stage >= 2:
                for g in range(4, 8):
                    sevac0(g)
            for k in range(NT if stage >= 2 else 0):
                par = k % 2
                if k >= 1:
                    w(vector, "a_xt", k + 1)
                    vector.tensor_copy(xnT[:], psX[:]).then_inc(S["a_xe"], 1)
                if k >= 2:
                    w(vector, "a_vd", k - 1)
                    w(vector, "d_s", 16 * (k - 3))
                    vector.tensor_scalar(sgbuf[:, k % 2, 0:512], psV[:, 0:512],
                                         stA[:, k % 2, A_RL:A_RL + 1], None, OP.mult)
                    vector.tensor_scalar(sgbuf[:, k % 2, 512:1024], psV[:, 512:1024],
                                         stA[:, k % 2, A_RL:A_RL + 1], None, OP.mult
                                         ).then_inc(S["a_ve"], 1)
                for kg in (range(8) if k >= 1 else []):
                    if kg == 0:
                        w(vector, "a_pe", k - 1)
                    w(vector, "a_sd", 8 * k + kg + 1)
                    vector.tensor_scalar(SS[par][:, kg * 512:(kg + 1) * 512],
                                         psA[:, kg % 2, :], 0.0, NEG, OP.add, OP.max,
                                         accum_out=mpt[:, kg:kg + 1]
                                         ).then_inc(S["a_se"], 1)
                if k >= 1:
                    w(vector, "a_pe", k)
                    vector.reciprocal(stA[:, 1 - par, A_RL:A_RL + 1],
                                      stA[:, 1 - par, A_L:A_L + 1]).then_inc(S["a_rl"], 1)
                    vector.drain()
                    for tb in range(4):
                        w(vector, "a_pt", 4 * (k - 1) + tb + 1)
                        vector.tensor_copy(PT8[:, tb * 1024:(tb + 1) * 1024],
                                           psT[:, tb % 2, :]).then_inc(S["a_pv"], 1)
                vector.drain()
                vector.reduce_max(stA[:, par, A_NM:A_NM + 1], mpt[:], axis=AX,
                                  negate=True).then_inc(S["a_me"], 1)
                if k + 1 < NT:
                    w(vector, "d_x", 16 * (k + 2))
                    vector.tensor_scalar(scrb[:], xbuf[:, (k + 1) % 2, :], 0.0, 0.0,
                                         OP.add, OP.add,
                                         accum_out=stA[:, (k + 1) % 2, A_SUM:A_SUM + 1])
                    w(vector, "a_sq", k + 2)
                    newton_rsqrt(vector, stA, (k + 1) % 2, stP[:, k + 1, 0:1],
                                 stP[:, k + 1, 1:2], out_rs12=stP[:, k + 1, 2:3],
                                 out_nmr12=stP[:, k + 1, 3:4]).then_inc(S["a_ln"], 1)
            if stage >= 2:
                par = (NT - 1) % 2
                w(vector, "a_pe", NT)
                vector.reciprocal(stA[:, par, A_RL:A_RL + 1], stA[:, par, A_L:A_L + 1]
                                  ).then_inc(S["a_rl"], 1)
                vector.drain()
                for tb in range(4):
                    w(vector, "a_pt", 4 * (NT - 1) + tb + 1)
                    vector.tensor_copy(PT8[:, tb * 1024:(tb + 1) * 1024], psT[:, tb % 2, :]
                                       ).then_inc(S["a_pv"], 1)
                for j in (NT - 2, NT - 1):
                    w(vector, "a_vd", j + 1)
                    w(vector, "d_s", 16 * (j - 1))
                    vector.tensor_scalar(sgbuf[:, j % 2, 0:512], psV[:, 0:512],
                                         stA[:, j % 2, A_RL:A_RL + 1], None, OP.mult)
                    vector.tensor_scalar(sgbuf[:, j % 2, 512:1024], psV[:, 512:1024],
                                         stA[:, j % 2, A_RL:A_RL + 1], None, OP.mult
                                         ).then_inc(S["a_ve"], 1)
            if stage < 3:
                return
            w(vector, "a_wr", 64)
            vector.reduce_sum(accw[:, 0:1], wpart[:, 0:32], axis=AX)
            vector.reduce_sum(accw[:, 1:2], wpart[:, 32:64], axis=AX
                              ).then_inc(S["t_vw"], 1)
            w(vector, "t_pr", 2)
            vector.tensor_copy(wtmp[0:1, 0:2], psR[0:1, 0:2])
            vector.drain()
            vector.tensor_scalar(wtmp[0:1, 2:4], wtmp[0:1, 0:2], 1.0 / (D * H), EPS,
                                 OP.mult, OP.max).then_inc(S["t_vw"], 1)
            w(vector, "t_pb", 2)
            vector.tensor_copy(mwbc, psR[:, 2:4])
            vector.drain()
            vector.reciprocal(swbc, mwbc).then_inc(S["t_sw"], 1)
            vector.drain()
            for c in range(66):
                if c < 64:
                    col = 0 if c < 32 else 1
                    sem_n, cnt = qsem(c)
                    w(vector, sem_n, cnt)
                    src = qslot(c)
                    vector.tensor_scalar(src, src, swbc[:, col:col + 1],
                                         MAGIC, OP.mult, OP.add).then_inc(S["b_q1"], 1)
                if c >= 2:
                    cc = c - 2
                    w(vector, "b_q2", cc + 1)
                    dest = (KW1[:, cc % 8, (cc // 8) * 1024:(cc // 8 + 1) * 1024] if cc < 32
                            else VW2[:, cc - 32, :])
                    vector.tensor_scalar(dest, qslot(cc), -1.0, 1.0, OP.max, OP.min
                                         ).then_inc(S["b_qv"], 1)
            if stage < 4:
                return
            w(vector, "d_s", 16 * NT)
            w(vector, "a_wr", 64)
            w(vector, "a_pv", 4 * NT)
            w(vector, "d_bc", 16 * 4)
            w(vector, "d_wa", 16 * 64)

            def x2evac(im_, hb):
                w(vector, "b_pt", 8 * im_ + hb + 1)
                vector.tensor_copy(x2T[:, hb % 2, :], psT[:, hb % 2, 0:512]
                                   ).then_inc(S["b_pv"], 1)

            def q1_chain(j, ev=None):
                pj = j % 2
                sj = lambda kk: stB[:, pj, kk:kk + 1]

                def step():
                    if ev is not None:
                        hb = next(ev[1], None)
                        if hb is not None:
                            x2evac(ev[0], hb)
                w(vector, "d_x", 16 * (NT + j + 1))
                w(vector, "b_xq", j)
                vector.tensor_scalar(xn_, xbuf[:, j % 3, :], stP[:, j, 2:3],
                                     stP[:, j, 3:4], OP.mult, OP.add)
                step()
                vector.tensor_tensor(xn_, xn_, gbc[:], OP.mult)
                step()
                vector.tensor_tensor(xn_, xn_, bbc[:], OP.add)
                step()
                vector.reduce_max(sj(B_AM1), xn_, axis=AX, apply_absolute_value=True)
                step()
                vector.drain()
                vector.tensor_scalar(sj(B_M1), sj(B_AM1), EPS, None, OP.max)
                vector.drain()
                vector.reciprocal(sj(B_T), sj(B_M1))
                vector.drain()
                vector.tensor_scalar(sj(B_SX1), sj(B_T), 127.0, None, OP.mult)
                vector.tensor_tensor(sj(B_T), sj(B_M1), mwbc[:, 0:1], OP.mult)
                vector.drain()
                vector.tensor_scalar(sj(B_DS1), sj(B_T), 1.0 / 127.0, None, OP.mult)
                vector.drain()
                step()
                vector.tensor_scalar(xn_, xn_, sj(B_SX1), MAGIC, OP.mult, OP.add
                                     ).then_inc(S["b_q1a"], 1)

            q1_chain(0)
            w(vector, "b_xt", 1)
            vector.tensor_copy(xnT[:], psX[:]).then_inc(S["b_xe"], 1)
            for i in range(NT):
                par = i % 2
                sB = lambda kk: stB[:, par, kk:kk + 1]
                sBp = lambda kk: stB[:, 1 - par, kk:kk + 1]
                evs = list(range(8) if i == 0 else range(2, 8))
                ams = list(range(8))
                order = []
                while evs or ams:
                    if evs:
                        order.append(("e", evs.pop(0)))
                    if ams:
                        order.append(("a", ams.pop(0)))
                if i >= 1:
                    ai = [n for n, kg in enumerate(order) if kg == ("a", 6)]
                    order.insert(ai[0], ("h", 0))
                    aj = [n for n, kg in enumerate(order) if kg == ("a", 7)]
                    order.insert(aj[0], ("h", 1))
                for kind, g in order:
                    if kind == "h":
                        x2evac(i - 1, g)
                    elif kind == "e":
                        w(vector, "b_fd", 8 * i + g + 1)
                        if g == 0:
                            w(vector, "b_q2c", i)
                        vector.scalar_tensor_tensor(big0[:, g * 512:(g + 1) * 512],
                                                    psA[:, g % 2, :], sB(B_DS1),
                                                    bc16[:, g * 512:(g + 1) * 512],
                                                    OP.mult, OP.add).then_inc(S["b_fe"], 1)
                    else:
                        w(vector, "b_ge", 8 * i + g + 1)
                        vector.reduce_max(mpt[:, g:g + 1],
                                          big0[:, g * 512:(g + 1) * 512], axis=AX,
                                          apply_absolute_value=True)
                vector.drain()
                vector.reduce_max(sB(B_AM2), mpt[:], axis=AX)
                vector.drain()
                vector.tensor_scalar(sB(B_M2), sB(B_AM2), EPS, None, OP.max)
                vector.drain()
                vector.reciprocal(sB(B_T), sB(B_M2))
                vector.drain()
                vector.tensor_scalar(sB(B_SX2), sB(B_T), 127.0, None, OP.mult)
                vector.tensor_tensor(sB(B_T), sB(B_M2), mwbc[:, 1:2], OP.mult)
                vector.drain()
                vector.tensor_scalar(sB(B_DS2), sB(B_T), 1.0 / 127.0, None, OP.mult
                                     ).then_inc(S["b_q2s"], 1)
                vector.drain()
                it = iter(range(2, 8))
                if i + 1 < NT:
                    q1_chain(i + 1, ev=(i - 1, it) if i >= 1 else None)
                if i >= 1:
                    for hb in it:
                        x2evac(i - 1, hb)
                if i + 1 < NT:
                    w(vector, "b_xt", i + 2)
                    vector.tensor_copy(xnT[:], psX[:]).then_inc(S["b_xe"], 1)
                if i >= 1:
                    w(vector, "b_vd", i)
                    vector.scalar_tensor_tensor(gt_[:, 0:512], psV[:, 0:512], sBp(B_DS2),
                                                b2bc[:, 0:512], OP.mult, OP.add)
                    vector.scalar_tensor_tensor(gt_[:, 512:1024], psV[:, 512:1024],
                                                sBp(B_DS2), b2bc[:, 512:1024],
                                                OP.mult, OP.add).then_inc(S["b_y2"], 1)
                if i + 1 < NT:
                    sN = lambda kk: stB[:, (i + 1) % 2, kk:kk + 1]
                    for hg in range(2):
                        if hg == 0:
                            w(vector, "b_q2c", i + 1)
                        w(vector, "b_fd", 8 * (i + 1) + hg + 1)
                        vector.scalar_tensor_tensor(big0[:, hg * 512:(hg + 1) * 512],
                                                    psA[:, hg % 2, :], sN(B_DS1),
                                                    bc16[:, hg * 512:(hg + 1) * 512],
                                                    OP.mult, OP.add).then_inc(S["b_fe"], 1)
                if i >= 1:
                    w(vector, "b_sl", i)
                    w(vector, "d_sr", 16 * i)
                    vector.tensor_tensor(gt_, gt_, sgbf[:, (i - 1) % 2, :], OP.mult)
                    vector.tensor_tensor(xbuf[:, (i - 1) % 3, :], xbuf[:, (i - 1) % 3, :],
                                         gt_, OP.add).then_inc(S["b_fin"], 1)
            par = (NT - 1) % 2
            sB = lambda kk: stB[:, par, kk:kk + 1]
            for hb in range(8):
                w(vector, "b_pt", 8 * (NT - 1) + hb + 1)
                vector.tensor_copy(x2T[:, hb % 2, :], psT[:, hb % 2, 0:512]
                                   ).then_inc(S["b_pv"], 1)
            w(vector, "b_vd", NT)
            vector.scalar_tensor_tensor(gt_[:, 0:512], psV[:, 0:512], sB(B_DS2),
                                        b2bc[:, 0:512], OP.mult, OP.add)
            vector.scalar_tensor_tensor(gt_[:, 512:1024], psV[:, 512:1024], sB(B_DS2),
                                        b2bc[:, 512:1024], OP.mult, OP.add
                                        ).then_inc(S["b_y2"], 1)
            w(vector, "b_sl", NT)
            w(vector, "d_sr", 16 * NT)
            vector.tensor_tensor(gt_, gt_, sgbf[:, (NT - 1) % 2, :], OP.mult)
            vector.tensor_tensor(xbuf[:, (NT - 1) % 3, :], xbuf[:, (NT - 1) % 3, :],
                                 gt_, OP.add).then_inc(S["b_fin"], 1)

        @blk.scalar
        def _(act):
            act.dma_start(out=kstg(1), in_=KTd[128:256, 0:1024]
                          ).then_inc(S[ksem(1)[0]], 16)
            act.dma_start(out=kstg(3), in_=KTd[384:512, 0:1024]
                          ).then_inc(S[ksem(3)[0]], 16)
            w(act, "p_s4a", 1)
            act.activation(scrb[0:4, :], xbuf[0:4, 0, :], AF.Square,
                           accum_out=stA[0:4, 0, A_SSQ:A_SSQ + 1]).then_inc(S["p_sq4"], 1)
            w(act, "p_ln4", 1)
            act.activation(xbuf[0:4, 0, :], xbuf[0:4, 0, :], AF.Identity,
                           bias=stA[0:4, 0, A_NM:A_NM + 1],
                           scale=stA[0:4, 0, A_Y:A_Y + 1]).then_inc(S["p_x4"], 1)
            w(act, "p_bce", 8)
            act.dma_start(out=kstg(5), in_=KTd[(5 % 8) * 128:(5 % 8 + 1) * 128, 0:1024]
                          ).then_inc(S[ksem(5)[0]], 16)
            if stage >= 2:
                w(act, "d_x", 16)
                act.activation(scrb[:], xbuf[:, 0, :], AF.Square,
                               accum_out=stA[:, 0, A_SSQ:A_SSQ + 1]).then_inc(S["a_sq"], 1)
                w(act, "a_ln", 1)
                act.activation(xnb[:], xbuf[:, 0, :], AF.Identity,
                               bias=stP[:, 0, 1:2], scale=stP[:, 0, 0:1]
                               ).then_inc(S["a_xa"], 1)
            w(act, "p_kc", 2)
            act.dma_start(out=kstg(7), in_=KTd[(7 % 8) * 128:(7 % 8 + 1) * 128, 0:1024]
                          ).then_inc(S[ksem(7)[0]], 16)
            for k in range(9, 32, 2):
                w(act, "p_kc", k - 5)
                act.dma_start(out=kstg(k),
                              in_=KTd[(k % 8) * 128:(k % 8 + 1) * 128,
                                      (k // 8) * 1024:(k // 8 + 1) * 1024]
                              ).then_inc(S[ksem(k)[0]], 16)
            for k in range(NT if stage >= 2 else 0):
                par = k % 2
                if k >= 1:
                    if k == 1:
                        w(act, "p_kc", 32)
                    w(act, "a_me", k)
                    w(act, "a_pt", 4 * (k - 1))
                    act.activation(P8[:], SS[1 - par][:], AF.Exp,
                                   bias=stA[:, 1 - par, A_NM:A_NM + 1],
                                   accum_out=stA[:, 1 - par, A_L:A_L + 1]
                                   ).then_inc(S["a_pe"], 1)
                if k + 1 < NT:
                    w(act, "d_x", 16 * (k + 2))
                    act.activation(scrb[:], xbuf[:, (k + 1) % 2, :], AF.Square,
                                   accum_out=stA[:, (k + 1) % 2, A_SSQ:A_SSQ + 1]
                                   ).then_inc(S["a_sq"], 1)
                    w(act, "a_ln", k + 2)
                    w(act, "a_xt", k + 1)
                    act.activation(xnb[:], xbuf[:, (k + 1) % 2, :], AF.Identity,
                                   bias=stP[:, k + 1, 1:2], scale=stP[:, k + 1, 0:1]
                                   ).then_inc(S["a_xa"], 1)
                for kk in (2 * k, 2 * k + 1):
                    w(act, "d_wa", 16 * (kk + 1))
                    act.activation(scrb[:], absg[:, kk % 2, :], AF.Abs,
                                   accum_out=wpart[:, kk:kk + 1]).then_inc(S["a_wr"], 1)
                if k >= 2:
                    w(act, "a_ve", k - 1)
                    act.dma_start(out=OUTd[128 * (k - 2):128 * (k - 1), :],
                                  in_=sgbuf[:, k % 2, :]).then_inc(S["d_s"], 16)
            if stage >= 2:
                par = (NT - 1) % 2
                w(act, "a_me", NT)
                w(act, "a_pt", 4 * (NT - 1))
                act.activation(P8[:], SS[par][:], AF.Exp, bias=stA[:, par, A_NM:A_NM + 1],
                               accum_out=stA[:, par, A_L:A_L + 1]).then_inc(S["a_pe"], 1)
                for j in (NT - 2, NT - 1):
                    w(act, "a_ve", j + 1)
                    act.dma_start(out=OUTd[128 * j:128 * (j + 1), :],
                                  in_=sgbuf[:, j % 2, :]).then_inc(S["d_s"], 16)
            if stage >= 3:
                act.dma_start(out=qslot(1), in_=wchunk_src(1)
                              ).then_inc(S[qsem(1)[0]], 16)
                act.dma_start(out=qslot(3), in_=wchunk_src(3)
                              ).then_inc(S[qsem(3)[0]], 16)
            for j in range(64 if stage >= 3 else 0):
                w(act, "b_q1", j + 1)
                act.activation(qslot(j), qslot(j), AF.Copy, bias=-MAGIC
                               ).then_inc(S["b_q2"], 1)
                c = j + 6
                if c < 64 and qname(c) == "d_pb3":
                    if c == 11:
                        w(act, "a_pv", 4 * NT)
                    w(act, "b_qv", c - 5)
                    act.dma_start(out=qslot(c), in_=wchunk_src(c)
                                  ).then_inc(S[qsem(c)[0]], 16)
            if stage < 4:
                return
            w(act, "b_q1a", 1)
            act.activation(xnb[:], xn_, AF.Copy, bias=-MAGIC).then_inc(S["b_xq"], 1)
            for i in range(NT):
                par = i % 2
                for hg in range(8):
                    w(act, "b_fe", 8 * i + hg + 1)
                    act.activation(big0[:, hg * 512:(hg + 1) * 512],
                                   big0[:, hg * 512:(hg + 1) * 512], AF.Gelu
                                   ).then_inc(S["b_ge"], 1)
                w(act, "b_q2s", i + 1)
                act.activation(big0[:], big0[:], AF.Identity,
                               bias=magicc, scale=stB[:, par, B_SX2:B_SX2 + 1])
                if i + 1 < NT:
                    w(act, "b_q1a", i + 2)
                    w(act, "b_xt", i + 1)
                    act.activation(xnb[:], xn_, AF.Copy, bias=-MAGIC).then_inc(S["b_xq"], 1)
                w(act, "b_pt", 8 * i)
                act.activation(x2q[:, :, :], big0[:], AF.Copy, bias=-MAGIC
                               ).then_inc(S["b_q2c"], 1)
                if i >= 1:
                    w(act, "b_y2", i)
                    act.activation(gt_, gt_, AF.Silu).then_inc(S["b_sl"], 1)
            w(act, "b_y2", NT)
            act.activation(gt_, gt_, AF.Silu).then_inc(S["b_sl"], 1)

        @blk.tensor
        def _(pe):
            w(pe, "g_init", 4)
            w(pe, "d_cat", 64)
            for g in range(8):
                w(pe, "p_bce", g - 1)
                pe.matmul(psA[:, g % 2, :], onesr[0:1, 0:128],
                          big0[0:1, g * 512:(g + 1) * 512], start=True, stop=True
                          ).then_inc(S["p_bc"], 1)
            for k in range(NT if stage >= 2 else 0):
                par = k % 2
                w(pe, "a_xa", k + 1)
                w(pe, "a_xe", k)
                if k == 0:
                    w(pe, "v_init", 1)
                for dc in range(8):
                    last = pe.transpose(psX[:, dc * 128:(dc + 1) * 128],
                                        xnb[:, dc * 128:(dc + 1) * 128], ident[:])
                last.then_inc(S["a_xt"], 1)
                for kg in range(8):
                    if k == 0:
                        if kg == 0:
                            w(pe, "p_bce", 8)
                        w(pe, "p_kc", 8 * (kg // 2 + 1))
                    if kg == 0:
                        if k == 1:
                            w(pe, "p_kc", 32)
                        w(pe, "a_xe", k + 1)
                    w(pe, "a_se", 8 * k + kg - 1)
                    for dc in range(8):
                        last = pe.matmul(psA[:, kg % 2, :], xnT[:, dc * 128:(dc + 1) * 128],
                                         KW1[:, dc, kg * 512:(kg + 1) * 512],
                                         start=(dc == 0), stop=(dc == 7))
                    last.then_inc(S["a_sd"], 1)
                if k >= 1:
                    km = k - 1
                    for tb in range(4):
                        if tb == 0:
                            w(pe, "a_pe", k)
                        w(pe, "a_pv", 4 * km + tb - 1)
                        for tc in range(8):
                            kc = 8 * tb + tc
                            last = pe.transpose(psT[:, tb % 2, tc * 128:(tc + 1) * 128],
                                                P8[:, kc * 128:(kc + 1) * 128], ident[:])
                        last.then_inc(S["a_pt"], 1)
                    w(pe, "a_ve", km)
                    for kc in range(32):
                        if km == 0:
                            w(pe, ("d_v", "d_va", "d_vb")[kc % 3], 16 * (kc // 3 + 1))
                        if kc % 8 == 0:
                            w(pe, "a_pv", 4 * km + kc // 8 + 1)
                        pe.matmul(psV[:, 0:512], PT8[:, kc * 128:(kc + 1) * 128],
                                  VW2[:, kc, 0:512], start=(kc == 0), stop=(kc == 31))
                        last = pe.matmul(psV[:, 512:1024], PT8[:, kc * 128:(kc + 1) * 128],
                                         VW2[:, kc, 512:1024], start=(kc == 0),
                                         stop=(kc == 31))
                    last.then_inc(S["a_vd"], 1)
            if stage >= 2:
                km = NT - 1
                for tb in range(4):
                    if tb == 0:
                        w(pe, "a_pe", NT)
                    w(pe, "a_pv", 4 * km + tb - 1)
                    for tc in range(8):
                        kc = 8 * tb + tc
                        last = pe.transpose(psT[:, tb % 2, tc * 128:(tc + 1) * 128],
                                            P8[:, kc * 128:(kc + 1) * 128], ident[:])
                    last.then_inc(S["a_pt"], 1)
                w(pe, "a_ve", km)
                for kc in range(32):
                    if kc % 8 == 0:
                        w(pe, "a_pv", 4 * km + kc // 8 + 1)
                    pe.matmul(psV[:, 0:512], PT8[:, kc * 128:(kc + 1) * 128],
                              VW2[:, kc, 0:512], start=(kc == 0), stop=(kc == 31))
                    last = pe.matmul(psV[:, 512:1024], PT8[:, kc * 128:(kc + 1) * 128],
                                     VW2[:, kc, 512:1024], start=(kc == 0), stop=(kc == 31))
                last.then_inc(S["a_vd"], 1)
            if stage < 3:
                return
            w(pe, "t_vw", 1)
            pe.matmul(psR[0:1, 0:1], accw[:, 0:1], ones, start=True, stop=True
                      ).then_inc(S["t_pr"], 1)
            pe.matmul(psR[0:1, 1:2], accw[:, 1:2], ones, start=True, stop=True
                      ).then_inc(S["t_pr"], 1)
            w(pe, "t_vw", 2)
            pe.matmul(psR[:, 2:3], onesr[0:1, :], wtmp[0:1, 2:3], start=True, stop=True
                      ).then_inc(S["t_pb"], 1)
            pe.matmul(psR[:, 3:4], onesr[0:1, :], wtmp[0:1, 3:4], start=True, stop=True
                      ).then_inc(S["t_pb"], 1)
            if stage < 4:
                return
            w(pe, "b_xq", 1)
            for dc in range(8):
                last = pe.transpose(psX[:, dc * 128:(dc + 1) * 128],
                                    xnb[:, dc * 128:(dc + 1) * 128], ident[:])
            last.then_inc(S["b_xt"], 1)
            for i in range(NT):
                par = i % 2
                for hg in range(8):
                    if i == 0:
                        w(pe, "b_qv", 8 * (hg // 2) + 8)
                    if hg == 0:
                        w(pe, "b_xe", i + 1)
                    w(pe, "b_fe", 8 * i + hg - 1)
                    for dc in range(8):
                        last = pe.matmul(psA[:, hg % 2, :], xnT[:, dc * 128:(dc + 1) * 128],
                                         KW1[:, dc, hg * 512:(hg + 1) * 512],
                                         start=(dc == 0), stop=(dc == 7))
                    last.then_inc(S["b_fd"], 1)
                if i == 0:
                    w(pe, "b_xq", 2)
                    for dc in range(8):
                        last = pe.transpose(psX[:, dc * 128:(dc + 1) * 128],
                                            xnb[:, dc * 128:(dc + 1) * 128], ident[:])
                    last.then_inc(S["b_xt"], 1)
                if i >= 1:
                    im = i - 1
                    if im == 0:
                        w(pe, "b_qv", 64)
                    w(pe, "b_y2", im)
                    for hb in range(9):
                        if hb < 8:
                            if hb == 0:
                                w(pe, "b_q2c", im + 1)
                            w(pe, "b_pv", 8 * im + hb - 1)
                            for c in range(4):
                                last = pe.transpose(psT[:, hb % 2, c * 128:(c + 1) * 128],
                                                    x2q[:, hb, c * 128:(c + 1) * 128],
                                                    ident[:])
                            last.then_inc(S["b_pt"], 1)
                        if hb == 8 and i + 1 < NT:
                            w(pe, "b_xq", i + 2)
                            for dc in range(8):
                                last = pe.transpose(psX[:, dc * 128:(dc + 1) * 128],
                                                    xnb[:, dc * 128:(dc + 1) * 128],
                                                    ident[:])
                            last.then_inc(S["b_xt"], 1)
                        if hb >= 1:
                            hq = hb - 1
                            w(pe, "b_pv", 8 * im + hq + 1)
                            for c in range(4):
                                kc = 4 * hq + c
                                pe.matmul(psV[:, 0:512], x2T[:, hq % 2, c * 128:(c + 1) * 128],
                                          VW2[:, kc, 0:512], start=(kc == 0), stop=(kc == 31))
                                last = pe.matmul(psV[:, 512:1024],
                                                 x2T[:, hq % 2, c * 128:(c + 1) * 128],
                                                 VW2[:, kc, 512:1024],
                                                 start=(kc == 0), stop=(kc == 31))
                    last.then_inc(S["b_vd"], 1)
            im = NT - 1
            w(pe, "b_y2", im)
            for hb in range(9):
                if hb < 8:
                    if hb == 0:
                        w(pe, "b_q2c", im + 1)
                    w(pe, "b_pv", 8 * im + hb - 1)
                    for c in range(4):
                        last = pe.transpose(psT[:, hb % 2, c * 128:(c + 1) * 128],
                                            x2q[:, hb, c * 128:(c + 1) * 128], ident[:])
                    last.then_inc(S["b_pt"], 1)
                if hb >= 1:
                    hq = hb - 1
                    w(pe, "b_pv", 8 * im + hq + 1)
                    for c in range(4):
                        kc = 4 * hq + c
                        pe.matmul(psV[:, 0:512], x2T[:, hq % 2, c * 128:(c + 1) * 128],
                                  VW2[:, kc, 0:512], start=(kc == 0), stop=(kc == 31))
                        last = pe.matmul(psV[:, 512:1024],
                                         x2T[:, hq % 2, c * 128:(c + 1) * 128],
                                         VW2[:, kc, 512:1024], start=(kc == 0),
                                         stop=(kc == 31))
            last.then_inc(S["b_vd"], 1)

    return nc


_NC_CACHE = []


def kernel(L0_attn_q, L0_attn_k, L0_attn_v, L0_attn_scale_factor, L0_attn_dropout_p,
           L0_ffn_w1, L0_ffn_b1, L0_ffn_w2, L0_ffn_b2, L0_ffn_gamma, L0_ffn_beta):
    q = np.ascontiguousarray(np.asarray(L0_attn_q))
    kT = np.ascontiguousarray(np.asarray(L0_attn_k)[0].T)
    v0 = np.ascontiguousarray(np.asarray(L0_attn_v)[0])
    w1T = np.ascontiguousarray(np.asarray(L0_ffn_w1).T)
    w2T = np.ascontiguousarray(np.asarray(L0_ffn_w2).T)
    b1bc = np.ascontiguousarray(np.broadcast_to(np.asarray(L0_ffn_b1)[None, :], (128, H)))
    b2bc = np.ascontiguousarray(np.broadcast_to(np.asarray(L0_ffn_b2)[None, :], (128, D)))
    gbc = np.ascontiguousarray(np.broadcast_to(np.asarray(L0_ffn_gamma)[None, :], (128, D)))
    bbc = np.ascontiguousarray(np.broadcast_to(np.asarray(L0_ffn_beta)[None, :], (128, D)))

    if not _NC_CACHE:
        _NC_CACHE.append(build_program())
    nc = _NC_CACHE[0]
    in_maps = [dict(x=np.ascontiguousarray(q[c]), kT=kT, v=v0, w1T=w1T, w2T=w2T,
                    b1bc=b1bc, b2bc=b2bc, gbc=gbc, bbc=bbc) for c in range(8)]
    res = run_bass_kernel_spmd(nc, in_maps, core_ids=list(range(8)))
    return np.stack([np.asarray(res.results[c]["out"]) for c in range(8)], axis=0)


# version 25
# speedup vs baseline: 1.0299x; 1.0101x over previous
import sys
if "/opt/trn_rl_repo" not in sys.path:
    sys.path.insert(0, "/opt/trn_rl_repo")
import numpy as np
import concourse.bass as bass
import concourse.mybir as mybir
from concourse.bass_utils import run_bass_kernel_spmd
from concourse.alu_op_type import AluOpType as OP

F32 = mybir.dt.float32
BF16 = mybir.dt.bfloat16
AF = mybir.ActivationFunctionType
AX = mybir.AxisListType.X

MAGIC = 1.5 * 2.0 ** 23
EPS = 1e-5
NT = 32
D = 1024
SK = 4096
H = 4096
NEG = -3.0e38


def build_program(stage=4):
    nc = bass.Bass("TRN2", target_bir_lowering=False, detect_race_conditions=False)

    Xd = nc.dram_tensor("x", [SK, D], F32, kind="ExternalInput").ap()
    KTd = nc.dram_tensor("kT", [D, SK], F32, kind="ExternalInput").ap()
    Vd = nc.dram_tensor("v", [SK, D], F32, kind="ExternalInput").ap()
    W1Td = nc.dram_tensor("w1T", [D, H], F32, kind="ExternalInput").ap()
    W2Td = nc.dram_tensor("w2T", [H, D], F32, kind="ExternalInput").ap()
    B1d = nc.dram_tensor("b1bc", [128, H], F32, kind="ExternalInput").ap()
    B2d = nc.dram_tensor("b2bc", [128, D], F32, kind="ExternalInput").ap()
    Gd = nc.dram_tensor("gbc", [128, D], F32, kind="ExternalInput").ap()
    Bd = nc.dram_tensor("bbc", [128, D], F32, kind="ExternalInput").ap()
    OUTd = nc.dram_tensor("out", [SK, D], F32, kind="ExternalOutput").ap()

    cur = [(nc.sbuf_base + 63) // 64 * 64]
    offs = {}

    def sb(name, shape, dt, alias_at=None):
        nbytes = int(np.prod(shape[1:])) * (2 if dt == BF16 else 4)
        off = cur[0] if alias_at is None else alias_at
        t = nc.alloc_sbuf_tensor_at(name, shape, dt, offset=off).ap()
        if alias_at is None:
            cur[0] += (nbytes + 31) // 32 * 32
        offs[name] = off
        return t

    KW1 = sb("KW1", [128, 8, SK], BF16)
    VW2 = sb("VW2", [128, 32, D], BF16)
    big0 = sb("big0", [128, SK], F32)
    bc16 = sb("bc16", [128, SK], F32)
    P8 = sb("P8", [128, SK], BF16)
    gbc = sb("gbc_s", [128, D], F32, alias_at=offs["P8"])
    bbc = sb("bbc_s", [128, D], F32, alias_at=offs["P8"] + 4096)
    stg45 = sb("stg45", [128, 2, 1024], F32, alias_at=offs["P8"])

    def kstg(k):
        s = k % 6
        if s < 2:
            return stg45[:, s, :]
        if s < 4:
            return stgPT[:, s - 2, :]
        return big0[:, (s - 2) * 1024:(s - 1) * 1024]

    def ksem(k):
        p = k // 2
        sems = ("d_ka", "d_kb", "d_kc") if k % 2 == 0 else ("d_kd", "d_ke", "d_kf")
        return sems[p % 3], 16 * (p // 3 + 1)

    def qslot(c):
        s = c % 6
        return big0[:, s * 1024:(s + 1) * 1024] if s < 4 else stg45[:, s - 4, :]

    QPAT = ("d_pb", "d_pb3", "d_pb", "d_pb3", "d_pb2")

    def qname(c):
        return QPAT[c % 5]

    def qpos(c):
        return sum(1 for j in range(c + 1) if QPAT[j % 5] == QPAT[c % 5])

    QSEMS = {"d_pb": ("d_pb", "d_pba", "d_pbb"),
             "d_pb3": ("d_pb3", "d_p3a", "d_p3b"),
             "d_pb2": ("d_pb2", "d_p2a")}

    def qsem(c):
        q = qname(c)
        p = qpos(c) - 1
        sems = QSEMS[q]
        return sems[p % len(sems)], 16 * (p // len(sems) + 1)
    PT8 = sb("PT8", [128, SK], BF16)
    x2q = sb("x2q", [128, 8, 512], BF16, alias_at=offs["PT8"])
    stgPT = sb("stgPT", [128, 2, 1024], F32, alias_at=offs["PT8"])
    xbuf = sb("xbuf", [128, 3, D], F32)
    sgbuf = sb("sgbuf", [128, 2, D], F32)
    sgbf = sb("sgbf", [128, 2, D], BF16, alias_at=offs["sgbuf"])
    x2T = sb("x2T", [128, 2, 512], BF16, alias_at=offs["sgbuf"] + 4096)
    xng = sb("xng", [128, D], F32)
    gtb = sb("gtb", [128, D], BF16, alias_at=offs["sgbuf"] + 6144)
    absg = sb("absg", [128, 2, D], BF16, alias_at=offs["xng"])
    xnb = sb("xnb", [128, D], BF16)
    xnT = sb("xnT", [128, D], BF16)
    scrb = sb("scrb", [128, D], BF16)
    b2bc = sb("b2bc_s", [128, D], BF16, alias_at=offs["scrb"])
    stP = sb("stP", [128, NT, 4], F32)
    stA = sb("stA", [128, 2, 9], F32)
    stB = sb("stB", [128, 2, 9], F32)
    mpt = sb("mpt", [128, 8], F32)
    accw = sb("accw", [128, 2], F32)
    wpart = sb("wpart", [128, 64], F32)
    wtmp = sb("wtmp", [128, 4], F32)
    msw = sb("msw", [128, 4], F32)
    mwbc = msw[:, 0:2]
    swbc = msw[:, 2:4]
    onesm = sb("onesm", [128, 2], F32)
    ones = onesm[:, 0:1]
    magicc = onesm[:, 1:2]
    onesr = sb("onesr", [1, 128], F32)
    ident = sb("ident", [128, 128], BF16)
    assert cur[0] <= 229376, f"SBUF over budget: {cur[0]}"

    xn_ = xng[:, 0:D]
    gt_ = gtb[:, 0:D]
    SS = [big0, bc16]

    psA = nc.alloc_psum_tensor("psA", [128, 2, 512], F32).ap()
    psV = nc.alloc_psum_tensor("psV", [128, 1024], F32).ap()
    psT = nc.alloc_psum_tensor("psT", [128, 2, 1024], BF16).ap()
    psX = nc.alloc_psum_tensor("psX", [128, 1024], BF16).ap()
    psR = nc.alloc_psum_tensor("psR", [128, 16], F32).ap()

    semnames = ("d_x d_pa d_v d_wa d_s d_pb d_bc d_sr d_o d_cat "
                "a_ln a_xa a_xt a_xe a_sd a_se a_me a_pe a_rl a_pt a_pv a_vd a_ve a_wr "
                "g_init v_init p_ln4 p_x4 p_bc p_bce p_kc "
                "t_pr t_vw t_pb t_sw b_q1 b_q2 b_qv "
                "b_l1a b_l2 b_h2 b_q1a b_xq b_xt b_xe b_fd b_fe b_ge b_q2s "
                "b_q2c b_pt b_pv b_vd b_y2 b_sl b_fin a_sq b_sq p_sq4 p_s4a d_pb2 "
                "d_pb3 d_pba d_pbb d_p3a d_p3b d_ka d_kb d_kc d_kd d_ke d_kf d_va d_vb d_p2a").split()
    S = {n: nc.alloc_semaphore(n) for n in semnames}

    def w(eng, name, v):
        if v > 0:
            eng.wait_ge(S[name], v)

    A_SUM, A_SSQ, A_MU, A_W, A_Y, A_T, A_NM, A_L, A_RL = 0, 1, 2, 3, 4, 5, 6, 7, 8
    B_DS2, B_T, B_AM1, B_M1, B_SX1, B_DS1, B_AM2, B_M2, B_SX2 = range(9)

    def newton_rsqrt(vector, st, par, out_rs, out_negmurs, npart=128, iters=3,
                 out_rs12=None, out_nmr12=None):
        sl = lambda k: st[0:npart, par, k:k + 1]
        dr = vector.drain
        dr()
        vector.tensor_scalar(sl(A_MU), sl(A_SUM), 1.0 / D, None, OP.mult)
        vector.tensor_scalar(sl(A_W), sl(A_SSQ), 1.0 / D, None, OP.mult)
        dr()
        vector.tensor_tensor(sl(A_T), sl(A_MU), sl(A_MU), OP.mult)
        dr()
        vector.tensor_tensor(sl(A_W), sl(A_W), sl(A_T), OP.subtract)
        dr()
        vector.tensor_scalar(sl(A_W), sl(A_W), EPS, None, OP.add)
        dr()
        vector.tensor_scalar(sl(A_Y), sl(A_W), -0.5, 1.5, OP.mult, OP.add)
        dr()
        for _ in range(iters - 1):
            vector.tensor_tensor(sl(A_T), sl(A_Y), sl(A_Y), OP.mult)
            dr()
            vector.tensor_tensor(sl(A_T), sl(A_T), sl(A_W), OP.mult)
            dr()
            vector.tensor_scalar(sl(A_T), sl(A_T), -0.5, 1.5, OP.mult, OP.add)
            dr()
            vector.tensor_tensor(sl(A_Y), sl(A_Y), sl(A_T), OP.mult)
            dr()
        vector.tensor_copy(out_rs, sl(A_Y))
        vector.tensor_tensor(sl(A_T), sl(A_MU), sl(A_Y), OP.mult)
        dr()
        last = vector.tensor_scalar(out_negmurs, sl(A_T), -1.0, None, OP.mult)
        if out_rs12 is None:
            return last
        vector.reciprocal(sl(A_MU), sl(A_W))
        dr()
        vector.tensor_scalar(sl(A_MU), sl(A_MU), -EPS, 1.0, OP.mult, OP.add)
        dr()
        vector.tensor_scalar(sl(A_MU), sl(A_MU), -0.5, 1.5 - 0.5 * EPS, OP.mult, OP.add)
        dr()
        vector.tensor_tensor(out_rs12, out_rs, sl(A_MU), OP.mult)
        return vector.tensor_tensor(out_nmr12, out_negmurs, sl(A_MU), OP.mult)

    def wchunk_src(k):
        if k < 32:
            dc, colblk = k % 8, k // 8
            return W1Td[128 * dc:128 * dc + 128,
                        1024 * colblk:1024 * colblk + 1024]
        j = k - 32
        return W2Td[128 * j:128 * j + 128, :]

    X4view = Xd.rearrange("(a b) d -> a b d", b=1024)[:, 0, :]

    with nc.Block() as blk:
        @blk.sync
        def _(sync):
            sync.dma_start(out=xbuf[0:4, 0, :], in_=X4view).then_inc(S["d_pa"], 16)
            if stage >= 2:
                w(sync, "d_cat", 64)
                sync.dma_start(out=xbuf[:, 0, :], in_=Xd[0:128, :]).then_inc(S["d_x"], 16)
                w(sync, "d_x", 16)
                sync.dma_start(out=xbuf[:, 1, :], in_=Xd[128:256, :]).then_inc(S["d_x"], 16)
            for k in range(0, 32, 2):
                if k == 4:
                    w(sync, "p_bce", 8)
                w(sync, "p_kc", k - 5)
                sync.dma_start(out=kstg(k),
                               in_=KTd[(k % 8) * 128:(k % 8 + 1) * 128,
                                       (k // 8) * 1024:(k // 8 + 1) * 1024]
                               ).then_inc(S[ksem(k)[0]], 16)
            for i in (range(2, NT) if stage >= 2 else []):
                w(sync, "a_xa", i - 1)
                w(sync, "d_x", 16 * i)
                sync.dma_start(out=xbuf[:, i % 2, :], in_=Xd[128 * i:128 * (i + 1), :]
                               ).then_inc(S["d_x"], 16)
            for c in range(64 if stage >= 3 else 0):
                if qname(c) != "d_pb":
                    continue
                if c == 0:
                    w(sync, "a_pe", NT - 1)
                if c == 5:
                    w(sync, "a_pv", 4 * NT)
                w(sync, "b_qv", c - 5)
                sync.dma_start(out=qslot(c), in_=wchunk_src(c)
                               ).then_inc(S[qsem(c)[0]], 16)
            for i in range(NT if stage >= 4 else 0):
                w(sync, "d_o", 16 * (i - 2))
                w(sync, "d_x", 16 * (NT + i))
                sync.dma_start(out=xbuf[:, i % 3, :], in_=Xd[128 * i:128 * (i + 1), :]
                               ).then_inc(S["d_x"], 16)

        @blk.gpsimd
        def _(gps):
            gps.iota(ident[:], [[1, 128]], channel_multiplier=-1,
                     allow_small_or_imprecise_dtypes=True).then_inc(S["g_init"], 1)
            gps.memset(onesm[:, 0:1], 1.0).then_inc(S["g_init"], 1)
            gps.memset(onesr[:], 1.0).then_inc(S["g_init"], 1)
            gps.memset(onesm[:, 1:2], MAGIC).then_inc(S["g_init"], 1)
            w(gps, "p_x4", 1)
            for j in range(4):
                gps.dma_start(out=big0[0:1, j * 1024:(j + 1) * 1024],
                              in_=xbuf[j:j + 1, 0, 0:1024]).then_inc(S["d_cat"], 16)
            for kc in range(32):
                w(gps, ("d_v", "d_va", "d_vb")[kc % 3], 16 * (kc // 3))
                gps.dma_start(out=VW2[:, kc, :], in_=Vd[128 * kc:128 * kc + 128, :]
                              ).then_inc(S[("d_v", "d_va", "d_vb")[kc % 3]], 16)
            for i in range(NT if stage >= 2 else 0):
                for k in (2 * i, 2 * i + 1):
                    w(gps, "a_wr", k)
                    gps.dma_start(out=absg[:, k % 2, :], in_=wchunk_src(k)
                                  ).then_inc(S["d_wa"], 16)
            if stage >= 3:
                for c in range(4, 64, 5):
                    if c == 4:
                        w(gps, "a_pv", 4 * NT)
                    w(gps, "b_qv", c - 5)
                    gps.dma_start(out=qslot(c), in_=wchunk_src(c)
                                  ).then_inc(S[qsem(c)[0]], 16)
            if stage < 4:
                if stage >= 2:
                    gps.wait_ge(S["d_s"], 16 * NT)
                return
            w(gps, "a_pe", NT)
            w(gps, "a_vd", NT)
            w(gps, "a_pv", 4 * NT)
            w(gps, "a_wr", 64)
            w(gps, "d_s", 16 * NT)
            gps.dma_start(out=bc16[:], in_=B1d).then_inc(S["d_bc"], 16)
            w(gps, "b_qv", 60)
            gps.dma_start(out=gbc[:], in_=Gd).then_inc(S["d_bc"], 16)
            gps.dma_start(out=bbc[:], in_=Bd).then_inc(S["d_bc"], 16)
            gps.dma_start(out=b2bc[:], in_=B2d).then_inc(S["d_bc"], 16)
            for i in range(NT):
                w(gps, "b_fin", i - 1)
                w(gps, "d_sr", 16 * i)
                gps.dma_start(out=sgbf[:, i % 2, :], in_=OUTd[128 * i:128 * (i + 1), :]
                              ).then_inc(S["d_sr"], 16)
                if i >= 1:
                    w(gps, "b_fin", i)
                    w(gps, "d_o", 16 * (i - 1))
                    gps.dma_start(out=OUTd[128 * (i - 1):128 * i, :],
                                  in_=xbuf[:, (i - 1) % 3, :]).then_inc(S["d_o"], 16)
            w(gps, "b_fin", NT)
            w(gps, "d_o", 16 * (NT - 1))
            gps.dma_start(out=OUTd[128 * (NT - 1):128 * NT, :],
                          in_=xbuf[:, (NT - 1) % 3, :]).then_inc(S["d_o"], 16)
            gps.wait_ge(S["d_o"], 16 * NT)

        @blk.vector
        def _(vector):
            w(vector, "g_init", 4)
            vector.tensor_scalar(ident[:], ident[:], 0.0, None, OP.is_equal
                                 ).then_inc(S["v_init"], 1)
            w(vector, "d_pa", 16)
            vector.tensor_scalar(scrb[0:4, :], xbuf[0:4, 0, :], 0.0, 0.0, OP.add, OP.add,
                                 accum_out=stA[0:4, 0, A_SUM:A_SUM + 1]).then_inc(S["p_s4a"], 1)
            w(vector, "p_sq4", 1)
            newton_rsqrt(vector, stA, 0, stA[0:4, 0, A_Y:A_Y + 1],
                         stA[0:4, 0, A_NM:A_NM + 1], npart=4).then_inc(S["p_ln4"], 1)
            for g in range(8):
                w(vector, "p_bc", g + 1)
                vector.tensor_copy(bc16[:, g * 512:(g + 1) * 512], psA[:, g % 2, :]
                                   ).then_inc(S["p_bce"], 1)
            if stage >= 2:
                w(vector, "d_x", 16)
                vector.tensor_scalar(scrb[:], xbuf[:, 0, :], 0.0, 0.0, OP.add, OP.add,
                                     accum_out=stA[:, 0, A_SUM:A_SUM + 1])
                w(vector, "a_sq", 1)
                newton_rsqrt(vector, stA, 0, stP[:, 0, 0:1], stP[:, 0, 1:2],
                             out_rs12=stP[:, 0, 2:3], out_nmr12=stP[:, 0, 3:4]
                             ).then_inc(S["a_ln"], 1)

            def kbuild(k):
                cb, dc = k // 8, k % 8
                sem_n, cnt = ksem(k)
                w(vector, sem_n, cnt)
                w(vector, "p_bce", 2 * cb + 2)
                vector.tensor_tensor(KW1[:, dc, cb * 1024:(cb + 1) * 1024], kstg(k),
                                     bc16[:, cb * 1024:(cb + 1) * 1024], OP.mult
                                     ).then_inc(S["p_kc"], 1)

            def sevac0(g):
                w(vector, "a_sd", g + 1)
                vector.tensor_scalar(SS[0][:, g * 512:(g + 1) * 512],
                                     psA[:, g % 2, :], 0.0, NEG, OP.add, OP.max,
                                     accum_out=mpt[:, g:g + 1]).then_inc(S["a_se"], 1)

            for k in range(8):
                kbuild(k)
            if stage >= 2:
                w(vector, "a_xt", 1)
                vector.tensor_copy(xnT[:], psX[:]).then_inc(S["a_xe"], 1)
            for step in range(4):
                for k in range(8 + 4 * step, 12 + 4 * step):
                    kbuild(k)
                if stage >= 2:
                    sevac0(step)
            for k in range(24, 32):
                kbuild(k)
            if stage >= 2:
                for g in range(4, 8):
                    sevac0(g)
            for k in range(NT if stage >= 2 else 0):
                par = k % 2
                if k >= 1:
                    w(vector, "a_xt", k + 1)
                    vector.tensor_copy(xnT[:], psX[:]).then_inc(S["a_xe"], 1)
                if k >= 2:
                    w(vector, "a_vd", k - 1)
                    w(vector, "d_s", 16 * (k - 3))
                    vector.tensor_scalar(sgbuf[:, k % 2, 0:512], psV[:, 0:512],
                                         stA[:, k % 2, A_RL:A_RL + 1], None, OP.mult)
                    vector.tensor_scalar(sgbuf[:, k % 2, 512:1024], psV[:, 512:1024],
                                         stA[:, k % 2, A_RL:A_RL + 1], None, OP.mult
                                         ).then_inc(S["a_ve"], 1)
                for kg in (range(8) if k >= 1 else []):
                    if kg == 0:
                        w(vector, "a_pe", k - 1)
                    w(vector, "a_sd", 8 * k + kg + 1)
                    vector.tensor_scalar(SS[par][:, kg * 512:(kg + 1) * 512],
                                         psA[:, kg % 2, :], 0.0, NEG, OP.add, OP.max,
                                         accum_out=mpt[:, kg:kg + 1]
                                         ).then_inc(S["a_se"], 1)
                if k >= 1:
                    w(vector, "a_pe", k)
                    vector.reciprocal(stA[:, 1 - par, A_RL:A_RL + 1],
                                      stA[:, 1 - par, A_L:A_L + 1]).then_inc(S["a_rl"], 1)
                    vector.drain()
                    for tb in range(4):
                        w(vector, "a_pt", 4 * (k - 1) + tb + 1)
                        vector.tensor_copy(PT8[:, tb * 1024:(tb + 1) * 1024],
                                           psT[:, tb % 2, :]).then_inc(S["a_pv"], 1)
                vector.drain()
                vector.reduce_max(stA[:, par, A_NM:A_NM + 1], mpt[:], axis=AX,
                                  negate=True).then_inc(S["a_me"], 1)
                if k + 1 < NT:
                    w(vector, "d_x", 16 * (k + 2))
                    vector.tensor_scalar(scrb[:], xbuf[:, (k + 1) % 2, :], 0.0, 0.0,
                                         OP.add, OP.add,
                                         accum_out=stA[:, (k + 1) % 2, A_SUM:A_SUM + 1])
                    w(vector, "a_sq", k + 2)
                    newton_rsqrt(vector, stA, (k + 1) % 2, stP[:, k + 1, 0:1],
                                 stP[:, k + 1, 1:2], out_rs12=stP[:, k + 1, 2:3],
                                 out_nmr12=stP[:, k + 1, 3:4]).then_inc(S["a_ln"], 1)
            if stage >= 2:
                par = (NT - 1) % 2
                w(vector, "a_pe", NT)
                vector.reciprocal(stA[:, par, A_RL:A_RL + 1], stA[:, par, A_L:A_L + 1]
                                  ).then_inc(S["a_rl"], 1)
                vector.drain()
                for tb in range(4):
                    w(vector, "a_pt", 4 * (NT - 1) + tb + 1)
                    vector.tensor_copy(PT8[:, tb * 1024:(tb + 1) * 1024], psT[:, tb % 2, :]
                                       ).then_inc(S["a_pv"], 1)
                for j in (NT - 2, NT - 1):
                    w(vector, "a_vd", j + 1)
                    w(vector, "d_s", 16 * (j - 1))
                    vector.tensor_scalar(sgbuf[:, j % 2, 0:512], psV[:, 0:512],
                                         stA[:, j % 2, A_RL:A_RL + 1], None, OP.mult)
                    vector.tensor_scalar(sgbuf[:, j % 2, 512:1024], psV[:, 512:1024],
                                         stA[:, j % 2, A_RL:A_RL + 1], None, OP.mult
                                         ).then_inc(S["a_ve"], 1)
            if stage < 3:
                return
            w(vector, "a_wr", 64)
            vector.reduce_sum(accw[:, 0:1], wpart[:, 0:32], axis=AX)
            vector.reduce_sum(accw[:, 1:2], wpart[:, 32:64], axis=AX
                              ).then_inc(S["t_vw"], 1)
            w(vector, "t_pr", 2)
            vector.tensor_copy(wtmp[0:1, 0:2], psR[0:1, 0:2])
            vector.drain()
            vector.tensor_scalar(wtmp[0:1, 2:4], wtmp[0:1, 0:2], 1.0 / (D * H), EPS,
                                 OP.mult, OP.max).then_inc(S["t_vw"], 1)
            w(vector, "t_pb", 2)
            vector.tensor_copy(mwbc, psR[:, 2:4])
            vector.drain()
            vector.reciprocal(swbc, mwbc).then_inc(S["t_sw"], 1)
            vector.drain()
            for c in range(66):
                if c < 64:
                    col = 0 if c < 32 else 1
                    sem_n, cnt = qsem(c)
                    w(vector, sem_n, cnt)
                    src = qslot(c)
                    vector.tensor_scalar(src, src, swbc[:, col:col + 1],
                                         MAGIC, OP.mult, OP.add).then_inc(S["b_q1"], 1)
                if c >= 2:
                    cc = c - 2
                    w(vector, "b_q2", cc + 1)
                    dest = (KW1[:, cc % 8, (cc // 8) * 1024:(cc // 8 + 1) * 1024] if cc < 32
                            else VW2[:, cc - 32, :])
                    vector.tensor_scalar(dest, qslot(cc), -1.0, 1.0, OP.max, OP.min
                                         ).then_inc(S["b_qv"], 1)
            if stage < 4:
                return
            w(vector, "d_s", 16 * NT)
            w(vector, "a_wr", 64)
            w(vector, "a_pv", 4 * NT)
            w(vector, "d_bc", 16 * 4)
            w(vector, "d_wa", 16 * 64)

            def x2evac(im_, hb):
                w(vector, "b_pt", 8 * im_ + hb + 1)
                vector.tensor_copy(x2T[:, hb % 2, :], psT[:, hb % 2, 0:512]
                                   ).then_inc(S["b_pv"], 1)

            def q1_chain(j, ev=None):
                pj = j % 2
                sj = lambda kk: stB[:, pj, kk:kk + 1]

                def step():
                    if ev is not None:
                        hb = next(ev[1], None)
                        if hb is not None:
                            x2evac(ev[0], hb)
                w(vector, "d_x", 16 * (NT + j + 1))
                w(vector, "b_xq", j)
                vector.tensor_scalar(xn_, xbuf[:, j % 3, :], stP[:, j, 2:3],
                                     stP[:, j, 3:4], OP.mult, OP.add)
                step()
                vector.tensor_tensor(xn_, xn_, gbc[:], OP.mult)
                step()
                vector.tensor_tensor(xn_, xn_, bbc[:], OP.add)
                step()
                vector.reduce_max(sj(B_AM1), xn_, axis=AX, apply_absolute_value=True)
                step()
                vector.drain()
                vector.tensor_scalar(sj(B_M1), sj(B_AM1), EPS, None, OP.max)
                vector.drain()
                vector.reciprocal(sj(B_T), sj(B_M1))
                vector.drain()
                vector.tensor_scalar(sj(B_SX1), sj(B_T), 127.0, None, OP.mult)
                vector.tensor_tensor(sj(B_T), sj(B_M1), mwbc[:, 0:1], OP.mult)
                vector.drain()
                vector.tensor_scalar(sj(B_DS1), sj(B_T), 1.0 / 127.0, None, OP.mult)
                vector.drain()
                step()
                vector.tensor_scalar(xn_, xn_, sj(B_SX1), MAGIC, OP.mult, OP.add
                                     ).then_inc(S["b_q1a"], 1)

            q1_chain(0)
            w(vector, "b_xt", 1)
            vector.tensor_copy(xnT[:], psX[:]).then_inc(S["b_xe"], 1)
            for i in range(NT):
                par = i % 2
                sB = lambda kk: stB[:, par, kk:kk + 1]
                sBp = lambda kk: stB[:, 1 - par, kk:kk + 1]
                evs = list(range(8) if i == 0 else range(2, 8))
                ams = list(range(8))
                order = []
                while evs or ams:
                    if evs:
                        order.append(("e", evs.pop(0)))
                    if ams:
                        order.append(("a", ams.pop(0)))
                if i >= 1:
                    ai = [n for n, kg in enumerate(order) if kg == ("a", 6)]
                    order.insert(ai[0], ("h", 0))
                    aj = [n for n, kg in enumerate(order) if kg == ("a", 7)]
                    order.insert(aj[0], ("h", 1))
                for kind, g in order:
                    if kind == "h":
                        x2evac(i - 1, g)
                    elif kind == "e":
                        w(vector, "b_fd", 8 * i + g + 1)
                        if g == 0:
                            w(vector, "b_q2c", i)
                        vector.scalar_tensor_tensor(big0[:, g * 512:(g + 1) * 512],
                                                    psA[:, g % 2, :], sB(B_DS1),
                                                    bc16[:, g * 512:(g + 1) * 512],
                                                    OP.mult, OP.add).then_inc(S["b_fe"], 1)
                    else:
                        w(vector, "b_ge", 8 * i + g + 1)
                        vector.reduce_max(mpt[:, g:g + 1],
                                          big0[:, g * 512:(g + 1) * 512], axis=AX,
                                          apply_absolute_value=True)
                vector.drain()
                vector.reduce_max(sB(B_AM2), mpt[:], axis=AX)
                vector.drain()
                vector.tensor_scalar(sB(B_M2), sB(B_AM2), EPS, None, OP.max)
                vector.drain()
                vector.reciprocal(sB(B_T), sB(B_M2))
                vector.drain()
                vector.tensor_scalar(sB(B_SX2), sB(B_T), 127.0, None, OP.mult)
                vector.tensor_tensor(sB(B_T), sB(B_M2), mwbc[:, 1:2], OP.mult)
                vector.drain()
                vector.tensor_scalar(sB(B_DS2), sB(B_T), 1.0 / 127.0, None, OP.mult
                                     ).then_inc(S["b_q2s"], 1)
                vector.drain()
                it = iter(range(2, 8))
                if i + 1 < NT:
                    q1_chain(i + 1, ev=(i - 1, it) if i >= 1 else None)
                if i >= 1:
                    for hb in it:
                        x2evac(i - 1, hb)
                if i + 1 < NT:
                    w(vector, "b_xt", i + 2)
                    vector.tensor_copy(xnT[:], psX[:]).then_inc(S["b_xe"], 1)
                if i >= 1:
                    w(vector, "b_vd", i)
                    vector.scalar_tensor_tensor(gt_[:, 0:512], psV[:, 0:512], sBp(B_DS2),
                                                b2bc[:, 0:512], OP.mult, OP.add)
                    vector.scalar_tensor_tensor(gt_[:, 512:1024], psV[:, 512:1024],
                                                sBp(B_DS2), b2bc[:, 512:1024],
                                                OP.mult, OP.add).then_inc(S["b_y2"], 1)
                if i + 1 < NT:
                    sN = lambda kk: stB[:, (i + 1) % 2, kk:kk + 1]
                    for hg in range(2):
                        if hg == 0:
                            w(vector, "b_q2c", i + 1)
                        w(vector, "b_fd", 8 * (i + 1) + hg + 1)
                        vector.scalar_tensor_tensor(big0[:, hg * 512:(hg + 1) * 512],
                                                    psA[:, hg % 2, :], sN(B_DS1),
                                                    bc16[:, hg * 512:(hg + 1) * 512],
                                                    OP.mult, OP.add).then_inc(S["b_fe"], 1)
                if i >= 1:
                    w(vector, "b_sl", i)
                    w(vector, "d_sr", 16 * i)
                    vector.tensor_tensor(gt_, gt_, sgbf[:, (i - 1) % 2, :], OP.mult)
                    vector.tensor_tensor(xbuf[:, (i - 1) % 3, :], xbuf[:, (i - 1) % 3, :],
                                         gt_, OP.add).then_inc(S["b_fin"], 1)
            par = (NT - 1) % 2
            sB = lambda kk: stB[:, par, kk:kk + 1]
            for hb in range(8):
                w(vector, "b_pt", 8 * (NT - 1) + hb + 1)
                vector.tensor_copy(x2T[:, hb % 2, :], psT[:, hb % 2, 0:512]
                                   ).then_inc(S["b_pv"], 1)
            w(vector, "b_vd", NT)
            vector.scalar_tensor_tensor(gt_[:, 0:512], psV[:, 0:512], sB(B_DS2),
                                        b2bc[:, 0:512], OP.mult, OP.add)
            vector.scalar_tensor_tensor(gt_[:, 512:1024], psV[:, 512:1024], sB(B_DS2),
                                        b2bc[:, 512:1024], OP.mult, OP.add
                                        ).then_inc(S["b_y2"], 1)
            w(vector, "b_sl", NT)
            w(vector, "d_sr", 16 * NT)
            vector.tensor_tensor(gt_, gt_, sgbf[:, (NT - 1) % 2, :], OP.mult)
            vector.tensor_tensor(xbuf[:, (NT - 1) % 3, :], xbuf[:, (NT - 1) % 3, :],
                                 gt_, OP.add).then_inc(S["b_fin"], 1)

        @blk.scalar
        def _(act):
            act.dma_start(out=kstg(1), in_=KTd[128:256, 0:1024]
                          ).then_inc(S[ksem(1)[0]], 16)
            act.dma_start(out=kstg(3), in_=KTd[384:512, 0:1024]
                          ).then_inc(S[ksem(3)[0]], 16)
            w(act, "p_s4a", 1)
            act.activation(scrb[0:4, :], xbuf[0:4, 0, :], AF.Square,
                           accum_out=stA[0:4, 0, A_SSQ:A_SSQ + 1]).then_inc(S["p_sq4"], 1)
            w(act, "p_ln4", 1)
            act.activation(xbuf[0:4, 0, :], xbuf[0:4, 0, :], AF.Identity,
                           bias=stA[0:4, 0, A_NM:A_NM + 1],
                           scale=stA[0:4, 0, A_Y:A_Y + 1]).then_inc(S["p_x4"], 1)
            w(act, "p_bce", 8)
            act.dma_start(out=kstg(5), in_=KTd[(5 % 8) * 128:(5 % 8 + 1) * 128, 0:1024]
                          ).then_inc(S[ksem(5)[0]], 16)
            if stage >= 2:
                w(act, "d_x", 16)
                act.activation(scrb[:], xbuf[:, 0, :], AF.Square,
                               accum_out=stA[:, 0, A_SSQ:A_SSQ + 1]).then_inc(S["a_sq"], 1)
                w(act, "a_ln", 1)
                act.activation(xnb[:], xbuf[:, 0, :], AF.Identity,
                               bias=stP[:, 0, 1:2], scale=stP[:, 0, 0:1]
                               ).then_inc(S["a_xa"], 1)
            w(act, "p_kc", 2)
            act.dma_start(out=kstg(7), in_=KTd[(7 % 8) * 128:(7 % 8 + 1) * 128, 0:1024]
                          ).then_inc(S[ksem(7)[0]], 16)
            for k in range(9, 32, 2):
                w(act, "p_kc", k - 5)
                act.dma_start(out=kstg(k),
                              in_=KTd[(k % 8) * 128:(k % 8 + 1) * 128,
                                      (k // 8) * 1024:(k // 8 + 1) * 1024]
                              ).then_inc(S[ksem(k)[0]], 16)
            for k in range(NT if stage >= 2 else 0):
                par = k % 2
                if k >= 1:
                    if k == 1:
                        w(act, "p_kc", 32)
                    w(act, "a_me", k)
                    w(act, "a_pt", 4 * (k - 1))
                    act.activation(P8[:], SS[1 - par][:], AF.Exp,
                                   bias=stA[:, 1 - par, A_NM:A_NM + 1],
                                   accum_out=stA[:, 1 - par, A_L:A_L + 1]
                                   ).then_inc(S["a_pe"], 1)
                if k + 1 < NT:
                    w(act, "d_x", 16 * (k + 2))
                    act.activation(scrb[:], xbuf[:, (k + 1) % 2, :], AF.Square,
                                   accum_out=stA[:, (k + 1) % 2, A_SSQ:A_SSQ + 1]
                                   ).then_inc(S["a_sq"], 1)
                    w(act, "a_ln", k + 2)
                    w(act, "a_xt", k + 1)
                    act.activation(xnb[:], xbuf[:, (k + 1) % 2, :], AF.Identity,
                                   bias=stP[:, k + 1, 1:2], scale=stP[:, k + 1, 0:1]
                                   ).then_inc(S["a_xa"], 1)
                for kk in (2 * k, 2 * k + 1):
                    w(act, "d_wa", 16 * (kk + 1))
                    act.activation(scrb[:], absg[:, kk % 2, :], AF.Abs,
                                   accum_out=wpart[:, kk:kk + 1]).then_inc(S["a_wr"], 1)
                if k >= 2:
                    w(act, "a_ve", k - 1)
                    act.dma_start(out=OUTd[128 * (k - 2):128 * (k - 1), :],
                                  in_=sgbuf[:, k % 2, :]).then_inc(S["d_s"], 16)
            if stage >= 2:
                par = (NT - 1) % 2
                w(act, "a_me", NT)
                w(act, "a_pt", 4 * (NT - 1))
                act.activation(P8[:], SS[par][:], AF.Exp, bias=stA[:, par, A_NM:A_NM + 1],
                               accum_out=stA[:, par, A_L:A_L + 1]).then_inc(S["a_pe"], 1)
                for j in (NT - 2, NT - 1):
                    w(act, "a_ve", j + 1)
                    act.dma_start(out=OUTd[128 * j:128 * (j + 1), :],
                                  in_=sgbuf[:, j % 2, :]).then_inc(S["d_s"], 16)
            if stage >= 3:
                act.dma_start(out=qslot(1), in_=wchunk_src(1)
                              ).then_inc(S[qsem(1)[0]], 16)
                act.dma_start(out=qslot(3), in_=wchunk_src(3)
                              ).then_inc(S[qsem(3)[0]], 16)
            for j in range(64 if stage >= 3 else 0):
                w(act, "b_q1", j + 1)
                act.activation(qslot(j), qslot(j), AF.Copy, bias=-MAGIC
                               ).then_inc(S["b_q2"], 1)
                c = j + 6
                if c < 64 and qname(c) == "d_pb3":
                    if c == 11:
                        w(act, "a_pv", 4 * NT)
                    w(act, "b_qv", c - 5)
                    act.dma_start(out=qslot(c), in_=wchunk_src(c)
                                  ).then_inc(S[qsem(c)[0]], 16)
            if stage < 4:
                return
            w(act, "b_q1a", 1)
            act.activation(xnb[:], xn_, AF.Copy, bias=-MAGIC).then_inc(S["b_xq"], 1)
            for i in range(NT):
                par = i % 2
                for hg in range(8):
                    w(act, "b_fe", 8 * i + hg + 1)
                    act.activation(big0[:, hg * 512:(hg + 1) * 512],
                                   big0[:, hg * 512:(hg + 1) * 512], AF.Gelu
                                   ).then_inc(S["b_ge"], 1)
                w(act, "b_q2s", i + 1)
                act.activation(big0[:], big0[:], AF.Identity,
                               bias=magicc, scale=stB[:, par, B_SX2:B_SX2 + 1])
                if i + 1 < NT:
                    w(act, "b_q1a", i + 2)
                    w(act, "b_xt", i + 1)
                    act.activation(xnb[:], xn_, AF.Copy, bias=-MAGIC).then_inc(S["b_xq"], 1)
                w(act, "b_pt", 8 * i)
                act.activation(x2q[:, :, :], big0[:], AF.Copy, bias=-MAGIC
                               ).then_inc(S["b_q2c"], 1)
                if i >= 1:
                    w(act, "b_y2", i)
                    act.activation(gt_, gt_, AF.Silu).then_inc(S["b_sl"], 1)
            w(act, "b_y2", NT)
            act.activation(gt_, gt_, AF.Silu).then_inc(S["b_sl"], 1)

        @blk.tensor
        def _(pe):
            w(pe, "g_init", 4)
            w(pe, "d_cat", 64)
            for g in range(8):
                w(pe, "p_bce", g - 1)
                pe.matmul(psA[:, g % 2, :], onesr[0:1, 0:128],
                          big0[0:1, g * 512:(g + 1) * 512], start=True, stop=True
                          ).then_inc(S["p_bc"], 1)
            for k in range(NT if stage >= 2 else 0):
                par = k % 2
                w(pe, "a_xa", k + 1)
                w(pe, "a_xe", k)
                if k == 0:
                    w(pe, "v_init", 1)
                for dc in range(8):
                    last = pe.transpose(psX[:, dc * 128:(dc + 1) * 128],
                                        xnb[:, dc * 128:(dc + 1) * 128], ident[:])
                last.then_inc(S["a_xt"], 1)
                for kg in range(8):
                    if k == 0:
                        if kg == 0:
                            w(pe, "p_bce", 8)
                        w(pe, "p_kc", 8 * (kg // 2 + 1))
                    if kg == 0:
                        if k == 1:
                            w(pe, "p_kc", 32)
                        w(pe, "a_xe", k + 1)
                    w(pe, "a_se", 8 * k + kg - 1)
                    for dc in range(8):
                        last = pe.matmul(psA[:, kg % 2, :], xnT[:, dc * 128:(dc + 1) * 128],
                                         KW1[:, dc, kg * 512:(kg + 1) * 512],
                                         start=(dc == 0), stop=(dc == 7))
                    last.then_inc(S["a_sd"], 1)
                if k >= 1:
                    km = k - 1
                    for tb in range(4):
                        if tb == 0:
                            w(pe, "a_pe", k)
                        w(pe, "a_pv", 4 * km + tb - 1)
                        for tc in range(8):
                            kc = 8 * tb + tc
                            last = pe.transpose(psT[:, tb % 2, tc * 128:(tc + 1) * 128],
                                                P8[:, kc * 128:(kc + 1) * 128], ident[:])
                        last.then_inc(S["a_pt"], 1)
                    w(pe, "a_ve", km)
                    for kc in range(32):
                        if km == 0:
                            w(pe, ("d_v", "d_va", "d_vb")[kc % 3], 16 * (kc // 3 + 1))
                        if kc % 8 == 0:
                            w(pe, "a_pv", 4 * km + kc // 8 + 1)
                        pe.matmul(psV[:, 0:512], PT8[:, kc * 128:(kc + 1) * 128],
                                  VW2[:, kc, 0:512], start=(kc == 0), stop=(kc == 31))
                        last = pe.matmul(psV[:, 512:1024], PT8[:, kc * 128:(kc + 1) * 128],
                                         VW2[:, kc, 512:1024], start=(kc == 0),
                                         stop=(kc == 31))
                    last.then_inc(S["a_vd"], 1)
            if stage >= 2:
                km = NT - 1
                for tb in range(4):
                    if tb == 0:
                        w(pe, "a_pe", NT)
                    w(pe, "a_pv", 4 * km + tb - 1)
                    for tc in range(8):
                        kc = 8 * tb + tc
                        last = pe.transpose(psT[:, tb % 2, tc * 128:(tc + 1) * 128],
                                            P8[:, kc * 128:(kc + 1) * 128], ident[:])
                    last.then_inc(S["a_pt"], 1)
                w(pe, "a_ve", km)
                for kc in range(32):
                    if kc % 8 == 0:
                        w(pe, "a_pv", 4 * km + kc // 8 + 1)
                    pe.matmul(psV[:, 0:512], PT8[:, kc * 128:(kc + 1) * 128],
                              VW2[:, kc, 0:512], start=(kc == 0), stop=(kc == 31))
                    last = pe.matmul(psV[:, 512:1024], PT8[:, kc * 128:(kc + 1) * 128],
                                     VW2[:, kc, 512:1024], start=(kc == 0), stop=(kc == 31))
                last.then_inc(S["a_vd"], 1)
            if stage < 3:
                return
            w(pe, "t_vw", 1)
            pe.matmul(psR[0:1, 0:1], accw[:, 0:1], ones, start=True, stop=True
                      ).then_inc(S["t_pr"], 1)
            pe.matmul(psR[0:1, 1:2], accw[:, 1:2], ones, start=True, stop=True
                      ).then_inc(S["t_pr"], 1)
            w(pe, "t_vw", 2)
            pe.matmul(psR[:, 2:3], onesr[0:1, :], wtmp[0:1, 2:3], start=True, stop=True
                      ).then_inc(S["t_pb"], 1)
            pe.matmul(psR[:, 3:4], onesr[0:1, :], wtmp[0:1, 3:4], start=True, stop=True
                      ).then_inc(S["t_pb"], 1)
            if stage < 4:
                return
            w(pe, "b_xq", 1)
            for dc in range(8):
                last = pe.transpose(psX[:, dc * 128:(dc + 1) * 128],
                                    xnb[:, dc * 128:(dc + 1) * 128], ident[:])
            last.then_inc(S["b_xt"], 1)
            for i in range(NT):
                par = i % 2
                for hg in range(8):
                    if i == 0:
                        w(pe, "b_qv", 8 * (hg // 2) + 8)
                    if hg == 0:
                        w(pe, "b_xe", i + 1)
                    w(pe, "b_fe", 8 * i + hg - 1)
                    for dc in range(8):
                        last = pe.matmul(psA[:, hg % 2, :], xnT[:, dc * 128:(dc + 1) * 128],
                                         KW1[:, dc, hg * 512:(hg + 1) * 512],
                                         start=(dc == 0), stop=(dc == 7))
                    last.then_inc(S["b_fd"], 1)
                if i == 0:
                    w(pe, "b_xq", 2)
                    for dc in range(8):
                        last = pe.transpose(psX[:, dc * 128:(dc + 1) * 128],
                                            xnb[:, dc * 128:(dc + 1) * 128], ident[:])
                    last.then_inc(S["b_xt"], 1)
                if i >= 1:
                    im = i - 1
                    if im == 0:
                        w(pe, "b_qv", 64)
                    w(pe, "b_y2", im)
                    for hb in range(9):
                        if hb < 8:
                            if hb == 0:
                                w(pe, "b_q2c", im + 1)
                            w(pe, "b_pv", 8 * im + hb - 1)
                            for c in range(4):
                                last = pe.transpose(psT[:, hb % 2, c * 128:(c + 1) * 128],
                                                    x2q[:, hb, c * 128:(c + 1) * 128],
                                                    ident[:])
                            last.then_inc(S["b_pt"], 1)
                        if hb == 8 and i + 1 < NT:
                            w(pe, "b_xq", i + 2)
                            for dc in range(8):
                                last = pe.transpose(psX[:, dc * 128:(dc + 1) * 128],
                                                    xnb[:, dc * 128:(dc + 1) * 128],
                                                    ident[:])
                            last.then_inc(S["b_xt"], 1)
                        if hb >= 1:
                            hq = hb - 1
                            w(pe, "b_pv", 8 * im + hq + 1)
                            for c in range(4):
                                kc = 4 * hq + c
                                pe.matmul(psV[:, 0:512], x2T[:, hq % 2, c * 128:(c + 1) * 128],
                                          VW2[:, kc, 0:512], start=(kc == 0), stop=(kc == 31))
                                last = pe.matmul(psV[:, 512:1024],
                                                 x2T[:, hq % 2, c * 128:(c + 1) * 128],
                                                 VW2[:, kc, 512:1024],
                                                 start=(kc == 0), stop=(kc == 31))
                    last.then_inc(S["b_vd"], 1)
            im = NT - 1
            w(pe, "b_y2", im)
            for hb in range(9):
                if hb < 8:
                    if hb == 0:
                        w(pe, "b_q2c", im + 1)
                    w(pe, "b_pv", 8 * im + hb - 1)
                    for c in range(4):
                        last = pe.transpose(psT[:, hb % 2, c * 128:(c + 1) * 128],
                                            x2q[:, hb, c * 128:(c + 1) * 128], ident[:])
                    last.then_inc(S["b_pt"], 1)
                if hb >= 1:
                    hq = hb - 1
                    w(pe, "b_pv", 8 * im + hq + 1)
                    for c in range(4):
                        kc = 4 * hq + c
                        pe.matmul(psV[:, 0:512], x2T[:, hq % 2, c * 128:(c + 1) * 128],
                                  VW2[:, kc, 0:512], start=(kc == 0), stop=(kc == 31))
                        last = pe.matmul(psV[:, 512:1024],
                                         x2T[:, hq % 2, c * 128:(c + 1) * 128],
                                         VW2[:, kc, 512:1024], start=(kc == 0),
                                         stop=(kc == 31))
            last.then_inc(S["b_vd"], 1)

    return nc


_NC_CACHE = []


def kernel(L0_attn_q, L0_attn_k, L0_attn_v, L0_attn_scale_factor, L0_attn_dropout_p,
           L0_ffn_w1, L0_ffn_b1, L0_ffn_w2, L0_ffn_b2, L0_ffn_gamma, L0_ffn_beta):
    q = np.ascontiguousarray(np.asarray(L0_attn_q))
    kT = np.ascontiguousarray(np.asarray(L0_attn_k)[0].T)
    v0 = np.ascontiguousarray(np.asarray(L0_attn_v)[0])
    w1T = np.ascontiguousarray(np.asarray(L0_ffn_w1).T)
    w2T = np.ascontiguousarray(np.asarray(L0_ffn_w2).T)
    b1bc = np.ascontiguousarray(np.broadcast_to(np.asarray(L0_ffn_b1)[None, :], (128, H)))
    b2bc = np.ascontiguousarray(np.broadcast_to(np.asarray(L0_ffn_b2)[None, :], (128, D)))
    gbc = np.ascontiguousarray(np.broadcast_to(np.asarray(L0_ffn_gamma)[None, :], (128, D)))
    bbc = np.ascontiguousarray(np.broadcast_to(np.asarray(L0_ffn_beta)[None, :], (128, D)))

    if not _NC_CACHE:
        _NC_CACHE.append(build_program())
    nc = _NC_CACHE[0]
    in_maps = [dict(x=np.ascontiguousarray(q[c]), kT=kT, v=v0, w1T=w1T, w2T=w2T,
                    b1bc=b1bc, b2bc=b2bc, gbc=gbc, bbc=bbc) for c in range(8)]
    res = run_bass_kernel_spmd(nc, in_maps, core_ids=list(range(8)))
    return np.stack([np.asarray(res.results[c]["out"]) for c in range(8)], axis=0)
